# version 02 (final)
# baseline (speedup 1.0000x reference)
import math
from contextlib import ExitStack

import numpy as np
import concourse.bass as bass
import concourse.mybir as mybir
from concourse.bass_utils import run_bass_kernel_spmd

F32 = mybir.dt.float32
BF16 = mybir.dt.bfloat16
U32 = mybir.dt.uint32
AF = mybir.ActivationFunctionType
OP = mybir.AluOpType
AX = mybir.AxisListType.X

NCORES = 8
B, T, D = 4, 2048, 2048
H, HD = 16, 128
R = 1024
NT = R // 128
KCONV = 4
EPS_LN = 1e-5
EPS_GN = 1e-5
EPS_RMS = 1e-6
EPS_Q = 1e-5
NEG = -1e30
LN2 = 0.6931471805599453
INV_LN2 = 1.4426950408889634
TOPK = 256
INNER = 8192

_RUN_KW = dict(core_ids=list(range(NCORES)))



class Chained:

    def __init__(self, p, eng, sem):
        self._p, self._e, self._s = p, eng, sem

    def wait_ge(self, sem, val):
        self._e.wait_ge(sem, val)

    def raw(self):
        return self._e

    def __getattr__(self, name):
        fn = getattr(self._e, name)

        def wrap(*a, **k):
            inst = fn(*a, **k)
            c = self._p.inc(inst, self._s)
            self._e.wait_ge(self._p.sem(self._s), c)
            return inst
        return wrap


class Prog:

    def __init__(self, name):
        self.nc = bass.Bass("TRN2", target_bir_lowering=False)
        self.name = name
        self.ctx = ExitStack()
        self.sems = {}
        self.cnt = {}
        self.consts = {}
        self._uid = 0

    def uid(self):
        self._uid += 1
        return self._uid

    def sem(self, name):
        if name not in self.sems:
            self.sems[name] = self.ctx.enter_context(self.nc.semaphore(name))
            self.cnt[name] = 0
        return self.sems[name]

    def inc(self, inst, name, by=1):
        inst.then_inc(self.sem(name), by)
        self.cnt[name] += by
        return self.cnt[name]

    def sb(self, name, shape, dt=F32):
        return self.ctx.enter_context(self.nc.sbuf_tensor(name, shape, dt))

    def ps(self, name, shape, dt=F32):
        return self.ctx.enter_context(self.nc.psum_tensor(name, shape, dt))

    def din(self, name, shape, dt=F32):
        return self.nc.dram_tensor(name, list(shape), dt, kind="ExternalInput").ap()

    def dout(self, name, shape, dt=F32):
        return self.nc.dram_tensor(name, list(shape), dt, kind="ExternalOutput").ap()

    def dint(self, name, shape, dt=F32):
        return self.nc.dram_tensor(name, list(shape), dt, kind="Internal").ap()

    def chain(self, eng, sem):
        return Chained(self, eng, sem)


def ladder_floor_frac(v, dst_frac, tmp, bits, consts):
    raise NotImplementedError



def _cm(a):
    return np.ascontiguousarray(a)


def chunk_major_cols(vec):
    return _cm(vec.reshape(-1, 128).T)


def wslice_colblock(wT):
    return _cm(wT)



def build_launch_A():
    p = Prog("A")
    nc = p.nc
    xr = p.din("xr", [R, D])
    xh = p.din("xh", [3, D])
    wts = {n: p.din(n, [D, D]) for n in ["wqT", "wkT", "wvT", "wigT", "wogT", "wgeT"]}
    cwf = p.din("cwf", [128, 16 * KCONV])
    cbf = p.din("cbf", [128, 16])
    bigf = p.din("bigf", [128, 16])
    bogf = p.din("bogf", [128, 16])
    bgef = p.din("bgef", [128, 16])
    identb = p.din("identb", [128, 128], BF16)
    identf = p.din("identf", [128, 128])
    ones1r = p.din("ones1r", [1, 128])

    o_xn0 = p.dout("xn0", [R, D])
    o_proj = {n: p.dout(n, [D, R]) for n in ["qf", "kf", "vf", "igf", "ogf", "gmf"]}
    o_cmax = p.dout("cmax", [1, 1])

    CW = 1027

    with p.ctx:
        xt = [p.sb(f"xt{i}", [128, D]) for i in range(2)]
        xht = p.sb("xht", [3, D])
        sq = p.sb("sq", [128, D])
        xn = p.sb("xn", [128, D])
        xnb = p.sb("xnb", [128, D], BF16)
        XNF = p.sb("XNF", [128, 16 * CW], BF16)
        XCF = p.sb("XCF", [128, 16 * 1024], BF16)
        acc = p.sb("acc", [128, 1024])
        wst = [p.sb(f"wst{i}", [128, D], BF16) for i in range(2)]
        ev = [p.sb(f"ev{i}", [128, 512]) for i in range(2)]
        q512 = p.sb("q512", [128, 512])
        sq512 = p.sb("sq512", [128, 512])
        row512 = p.sb("row512", [1, 512])
        cid = p.sb("cid", [128, 128], BF16)
        cidf = p.sb("cidf", [128, 128])
        c1r = p.sb("c1r", [1, 128])
        conswb = p.sb("conswb", [128, 16 * KCONV])
        consb = p.sb("consb", [128, 64])
        onesc = p.sb("onesc", [128, 1])
        stat = p.sb("stat", [128, 8])
        cmcol = p.sb("cmcol", [128, 1])
        cm1 = p.sb("cm1", [1, 128])
        hstat = p.sb("hstat", [3, 8])

        psP = [p.ps(f"psP{i}", [128, 512]) for i in range(2)]
        psT = [p.ps(f"psT{i}", [128, 128], BF16) for i in range(2)]
        psS = p.ps("psS", [1, 512])
        psB = p.ps("psB", [128, 512])
        psX = p.ps("psX", [1, 128])

        blk = p.ctx.enter_context(nc.Block())
        for s in ["sx", "sxt0", "sxt1", "sw0", "sw1", "sev0", "sev1", "sxn",
                  "sv", "sa", "sg", "st", "smm", "so"]:
            p.sem(s)

        def vsec(f):
            @blk.vector
            def _(v):
                f(p.chain(v, "sv"))

        def asec(f):
            @blk.scalar
            def _(a):
                f(p.chain(a, "sa"))

        def gsec(f):
            @blk.gpsimd
            def _(g):
                f(g)

        tsec, ssec = blk.tensor, blk.sync
        mk = {}

        @ssec
        def _(s):
            for dst, srcap in [(xht[:], xh), (cid[:], identb), (cidf[:], identf),
                               (c1r[:], ones1r), (conswb[:], cwf),
                               (consb[:, 0:16], cbf), (consb[:, 16:32], bigf),
                               (consb[:, 32:48], bogf), (consb[:, 48:64], bgef)]:
                p.inc(s.dma_start(out=dst, in_=srcap), "sx", 16)
            mk["consts"] = p.cnt["sx"]

        def emit_xt_dma(j):
            @ssec
            def _(s):
                if j >= 2:
                    s.wait_ge(p.sem("sv"), mk[("xtfree", j - 2)])
                p.inc(s.dma_start(out=xt[j % 2][:],
                                  in_=xr[j * 128:(j + 1) * 128, :]),
                      f"sxt{j % 2}", 16)

        def ln_tile(tile_idx):
            P = 3 if tile_idx < 0 else 128
            src = xht if tile_idx < 0 else xt[tile_idx % 2]
            st = hstat if tile_idx < 0 else stat

            def f1(v):
                if tile_idx < 0:
                    v.wait_ge(p.sem("sx"), mk["consts"])
                    v.memset(cmcol[:], 0.0)
                    v.memset(onesc[:], 1.0)
                else:
                    v.wait_ge(p.sem(f"sxt{tile_idx % 2}"),
                              16 * (tile_idx // 2 + 1))
                    v.wait_ge(p.sem("sxn"), 16 * tile_idx)
                    v.wait_ge(p.sem("st"), p.cnt["st"])
                v.tensor_reduce(st[0:P, 0:1], src[0:P, :], AX, OP.add)
                mk["lnred"] = p.cnt["sv"]
            vsec(f1)

            def f2(a):
                a.wait_ge(p.sem("sv"), mk["lnred"])
                a.activation(sq[0:P, :], src[0:P, :], AF.Square,
                             accum_out=st[0:P, 1:2])
                mk["sqdone"] = p.cnt["sa"]
            asec(f2)

            def f3(v):
                v.wait_ge(p.sem("sa"), mk["sqdone"])
                v.tensor_scalar(st[0:P, 2:3], st[0:P, 0:1], 1.0 / D, None, OP.mult)
                v.tensor_tensor(st[0:P, 4:5], st[0:P, 2:3], st[0:P, 2:3], OP.mult)
                v.scalar_tensor_tensor(st[0:P, 3:4], st[0:P, 1:2], 1.0 / D,
                                       st[0:P, 4:5], OP.mult, OP.subtract)
                v.tensor_scalar(st[0:P, 3:4], st[0:P, 3:4], EPS_LN, None, OP.add)
                mk["var"] = p.cnt["sv"]
            vsec(f3)

            def f4(a):
                a.wait_ge(p.sem("sv"), mk["var"])
                a.activation(st[0:P, 5:6], st[0:P, 3:4], AF.Sqrt)
                mk["sd"] = p.cnt["sa"]
            asec(f4)

            def f5(v):
                v.wait_ge(p.sem("sa"), mk["sd"])
                v.reciprocal(st[0:P, 6:7], st[0:P, 5:6])
                v.tensor_scalar(xn[0:P, :], src[0:P, :], st[0:P, 2:3],
                                st[0:P, 6:7], OP.subtract, OP.mult)
                if tile_idx >= 0:
                    v.tensor_reduce(st[:, 7:8], xn[:], AX, OP.max,
                                    apply_absolute_value=True)
                    mk[("xtfree", tile_idx)] = p.cnt["sv"]
                    v.tensor_max(cmcol[:], cmcol[:], st[:, 7:8])
                v.tensor_copy(xnb[0:P, :], xn[0:P, :])
                mk["xnb"] = p.cnt["sv"]
            vsec(f5)

            if tile_idx >= 0:
                @ssec
                def _(s):
                    s.wait_ge(p.sem("sv"), mk["xnb"])
                    r0 = tile_idx * 128
                    p.inc(s.dma_start(out=o_xn0[r0:r0 + 128, :], in_=xn[:]),
                          "sxn", 16)
            if tile_idx + 2 < NT and tile_idx >= 0:
                emit_xt_dma(tile_idx + 2)

            evbase = p.cnt["sa"]
            vready = mk["xnb"]

            @tsec
            def _(t):
                t.wait_ge(p.sem("sv"), vready)
                for c in range(16):
                    t.wait_ge(p.sem("sa"), evbase + c - 1 if c >= 2 else evbase)
                    ident = cid[:] if P == 128 else cid[0:3, 0:3]
                    p.inc(t.transpose(psT[c % 2][0:128, 0:P],
                                      xnb[0:P, c * 128:(c + 1) * 128], ident),
                          "st")

            stbase = p.cnt["st"] - 16
            for c in range(16):
                def fe(a, c=c):
                    a.wait_ge(p.sem("st"), stbase + c + 1)
                    off = c * CW + (0 if tile_idx < 0 else 3 + tile_idx * 128)
                    a.copy(XNF[:, off:off + P], psT[c % 2][0:128, 0:P])
                asec(fe)

        emit_xt_dma(0)
        emit_xt_dma(1)
        ln_tile(-1)
        for j in range(NT):
            ln_tile(j)

        mk["evac_all"] = p.cnt["sa"]

        for c in range(16):
            def fc(v, c=c):
                if c == 0:
                    v.wait_ge(p.sem("sa"), mk["evac_all"])
                else:
                    v.wait_ge(p.sem("sa"), mk[("silu", c - 1)])
                b0 = c * CW
                v.tensor_scalar(acc[:], XNF[:, b0:b0 + 1024],
                                conswb[:, c * 4:c * 4 + 1], consb[:, c:c + 1],
                                OP.mult, OP.add)
                for j in range(1, KCONV):
                    v.scalar_tensor_tensor(acc[:], XNF[:, b0 + j:b0 + j + 1024],
                                           conswb[:, c * 4 + j:c * 4 + j + 1],
                                           acc[:], OP.mult, OP.add)
                mk[("conv", c)] = p.cnt["sv"]
            vsec(fc)

            def fs(a, c=c):
                a.wait_ge(p.sem("sv"), mk[("conv", c)])
                a.activation(XCF[:, c * 1024:(c + 1) * 1024], acc[:], AF.Silu)
                mk[("silu", c)] = p.cnt["sa"]
            asec(fs)

        mk["conv_all"] = p.cnt["sa"]

        specs = [
            ("wqT", "qf", "l2", XNF),
            ("wkT", "kf", "l2", XNF),
            ("wvT", "vf", "copy", XNF),
            ("wigT", "igf", "sig16", XCF),
            ("wogT", "ogf", "sig32", XCF),
            ("wgeT", "gmf", "sig48", XCF),
        ]
        mm_after = {}
        wi = 0
        for (wname, oname, epi, RHS) in specs:
            for o in range(16):
                buf = wi % 2

                def fw(g, wname=wname, o=o, buf=buf, wi=wi):
                    if wi >= 2:
                        g.wait_ge(p.sem("smm"), mm_after[wi - 2])
                    srcap = bass.AP(wts[wname].tensor, o * 128,
                                    [[D, 128], [128 * D, 16], [1, 128]])
                    dst = bass.AP(wst[buf], 0, [[D, 128], [128, 16], [1, 128]])
                    p.inc(g.dma_start(out=dst, in_=srcap), f"sw{buf}", 16)
                gsec(fw)

                for g2 in range(2):
                    pp = psP[g2]

                    @tsec
                    def _(t, o=o, buf=buf, wi=wi, RHS=RHS, g2=g2, pp=pp):
                        if g2 == 0:
                            t.wait_ge(p.sem(f"sw{buf}"), 16 * (wi // 2 + 1))
                            if wi == 0:
                                t.wait_ge(p.sem("sa"), mk["conv_all"])
                        if wi >= 1:
                            t.wait_ge(p.sem(f"sev{g2}"), 16 * wi)
                        for dd in range(16):
                            if RHS is XNF:
                                c0 = dd * CW + 3 + g2 * 512
                            else:
                                c0 = dd * 1024 + g2 * 512
                            inst = t.matmul(pp[:], wst[buf][:, dd * 128:(dd + 1) * 128],
                                            RHS[:, c0:c0 + 512],
                                            start=(dd == 0), stop=(dd == 15))
                        p.inc(inst, "smm")
                        if g2 == 1:
                            mm_after[wi] = p.cnt["smm"]

                    mmc = p.cnt["smm"]
                    if epi == "copy":
                        def fe(a, pp=pp, g2=g2, mmc=mmc):
                            a.wait_ge(p.sem("smm"), mmc)
                            a.copy(ev[g2][:], pp[:])
                            mk["store_rdy"] = ("sa", p.cnt["sa"])
                        asec(fe)
                    elif epi.startswith("sig"):
                        bcol = int(epi[3:])

                        def fe(a, pp=pp, g2=g2, mmc=mmc, o=o, bcol=bcol):
                            a.wait_ge(p.sem("smm"), mmc)
                            a.activation(ev[g2][:], pp[:], AF.Sigmoid,
                                         bias=consb[:, bcol + o:bcol + o + 1])
                            mk["store_rdy"] = ("sa", p.cnt["sa"])
                        asec(fe)
                    else:
                        def fe(a, pp=pp, mmc=mmc):
                            a.wait_ge(p.sem("smm"), mmc)
                            a.wait_ge(p.sem("sv"), mk.get("l2_ev_done", 0))
                            a.activation(sq512[:], pp[:], AF.Square)
                            a.copy(q512[:], pp[:])
                            mk["l2sq"] = p.cnt["sa"]
                        asec(fe)

                        @tsec
                        def _(t):
                            t.wait_ge(p.sem("sa"), mk["l2sq"])
                            p.inc(t.matmul(psS[:], onesc[:], sq512[:],
                                           start=True, stop=True), "smm")

                        def fq(a):
                            a.wait_ge(p.sem("smm"), p.cnt["smm"])
                            a.activation(row512[:], psS[:], AF.Sqrt)
                            mk["l2sd"] = p.cnt["sa"]
                        asec(fq)

                        def fr(v):
                            v.wait_ge(p.sem("sa"), mk["l2sd"])
                            v.tensor_scalar(row512[:], row512[:], 1e-12, None,
                                            OP.max)
                            v.reciprocal(row512[:], row512[:])
                            mk["l2rn"] = p.cnt["sv"]
                        vsec(fr)

                        @tsec
                        def _(t):
                            t.wait_ge(p.sem("sv"), mk["l2rn"])
                            p.inc(t.matmul(psB[:], c1r[:], row512[:],
                                           start=True, stop=True), "smm")

                        def fm(v, g2=g2):
                            v.wait_ge(p.sem("smm"), p.cnt["smm"])
                            v.tensor_tensor(ev[g2][:], q512[:], psB[:], OP.mult)
                            mk["store_rdy"] = ("sv", p.cnt["sv"])
                            mk["l2_ev_done"] = p.cnt["sv"]
                        vsec(fm)

                    @ssec
                    def _(s, oname=oname, o=o, g2=g2):
                        semn, cnt = mk["store_rdy"]
                        s.wait_ge(p.sem(semn), cnt)
                        dst = o_proj[oname][o * 128:(o + 1) * 128,
                                            g2 * 512:(g2 + 1) * 512]
                        p.inc(s.dma_start(out=dst, in_=ev[g2][:]),
                              f"sev{g2}", 16)
                wi += 1

        @tsec
        def _(t):
            t.wait_ge(p.sem("sv"), mk[("xtfree", NT - 1)] + 1)
            p.inc(t.matmul(psX[:], cmcol[:], cidf[:], start=True, stop=True),
                  "smm")

        def fcm(v):
            v.wait_ge(p.sem("smm"), p.cnt["smm"])
            v.tensor_copy(cm1[:], psX[:])
            v.tensor_reduce(cm1[0:1, 0:1], cm1[:], AX, OP.max)
            mk["cmax"] = p.cnt["sv"]
        vsec(fcm)

        @ssec
        def _(s):
            s.wait_ge(p.sem("sv"), mk["cmax"])
            p.inc(s.dma_start(out=o_cmax, in_=cm1[0:1, 0:1]), "so", 16)
            s.wait_ge(p.sem("so"), p.cnt["so"])
            s.wait_ge(p.sem("sev0"), p.cnt["sev0"])
            s.wait_ge(p.sem("sev1"), p.cnt["sev1"])
            s.wait_ge(p.sem("sxn"), p.cnt["sxn"])

    return p.nc


def prep_A(ins):
    x = np.asarray(ins["L0_attn_x"], np.float32).reshape(B * T, D)
    identb = np.eye(128, dtype=np.float32)
    import ml_dtypes
    identb = identb.astype(ml_dtypes.bfloat16)
    identf = np.eye(128, dtype=np.float32)
    ones1r = np.ones((1, 128), np.float32)
    shared = {
        "wqT": _cm(np.asarray(ins["L0_attn_Wq"], np.float32).T),
        "wkT": _cm(np.asarray(ins["L0_attn_Wk"], np.float32).T),
        "wvT": _cm(np.asarray(ins["L0_attn_Wv"], np.float32).T),
        "wigT": _cm(np.asarray(ins["L0_attn_Wig"], np.float32).T),
        "wogT": _cm(np.asarray(ins["L0_attn_Wog"], np.float32).T),
        "wgeT": _cm(np.repeat(np.asarray(ins["L0_attn_Wg"], np.float32), 128, axis=0).T),
        "cwf": _cm(np.asarray(ins["L0_attn_conv_w"], np.float32)
                   .reshape(16, 128, KCONV).transpose(1, 0, 2).reshape(128, 64)),
        "cbf": chunk_major_cols(np.asarray(ins["L0_attn_conv_b"], np.float32)),
        "bigf": chunk_major_cols(np.asarray(ins["L0_attn_b_ig"], np.float32)),
        "bogf": chunk_major_cols(np.asarray(ins["L0_attn_b_og"], np.float32)),
        "bgef": chunk_major_cols(np.repeat(np.asarray(ins["L0_attn_b_g"], np.float32), 128)),
        "identb": identb,
        "identf": identf,
        "ones1r": ones1r,
    }
    maps = []
    for c in range(NCORES):
        r0 = c * R
        xh = (np.zeros((3, D), np.float32) if c % 2 == 0
              else _cm(x[r0 - 3:r0, :]))
        m = dict(shared)
        m["xr"] = _cm(x[r0:r0 + R, :])
        m["xh"] = xh
        maps.append(m)
    return maps


def build_launch_B():
    p = Prog("B")
    nc = p.nc
    ins = {n: p.din(n, [1024, T]) for n in ["gm", "ig", "kk", "vv", "qq", "og"]}
    mnw = p.din("mnw", [128, 8])
    mnb = p.din("mnb", [128, 8])
    gnw = p.din("gnw", [128, 8])
    gnb = p.din("gnb", [128, 8])
    woT8 = p.din("woT8", [1024, D])
    ones1r = p.din("ones1r", [1, 128])
    o_s0p = p.dout("s0p", [D, T])

    with p.ctx:
        tl = {n: [p.sb(f"{n}{i}", [128, T]) for i in range(2)]
              for n in ["gm", "ig", "kk", "vv", "qq", "og"]}
        mem = p.sb("mem", [128, T])
        t1 = p.sb("t1", [128, T])
        sq512 = p.sb("sq512", [128, 512])
        sr0 = p.sb("sr0", [1, 512])
        sr1 = p.sb("sr1", [1, 512])
        sr2 = p.sb("sr2", [1, 512])
        sr3 = p.sb("sr3", [1, 512])
        OC = p.sb("OC", [128, 8 * T], BF16)
        wst = [p.sb(f"wst{i}", [128, 8 * 128], BF16) for i in range(2)]
        ev = [p.sb(f"ev{i}", [128, 512]) for i in range(2)]
        cb1 = p.sb("cb1", [128, 8])
        cb2 = p.sb("cb2", [128, 8])
        cb3 = p.sb("cb3", [128, 8])
        cb4 = p.sb("cb4", [128, 8])
        onesc = p.sb("onesc", [128, 1])
        c1r = p.sb("c1r", [1, 128])

        psS = p.ps("psS", [1, 512])
        psQ = p.ps("psQ", [1, 512])
        psB1 = p.ps("psB1", [128, 512])
        psB2 = p.ps("psB2", [128, 512])
        psP = [p.ps(f"psP{i}", [128, 512]) for i in range(2)]

        blk = p.ctx.enter_context(nc.Block())
        for s in ["sx", "sv", "sa", "st", "smm", "sxt0", "sxt1", "sw0", "sw1",
                  "sev0", "sev1", "so"]:
            p.sem(s)

        def vsec(f):
            @blk.vector
            def _(v):
                f(p.chain(v, "sv"))

        def asec(f):
            @blk.scalar
            def _(a):
                f(p.chain(a, "sa"))

        tsec, ssec, gsec = blk.tensor, blk.sync, blk.gpsimd
        mk = {}

        @ssec
        def _(s):
            for dst, srcap in [(cb1[:], mnw), (cb2[:], mnb), (cb3[:], gnw),
                               (cb4[:], gnb), (c1r[:], ones1r)]:
                p.inc(s.dma_start(out=dst, in_=srcap), "sx", 16)
            mk["consts"] = p.cnt["sx"]

        def emit_head_dma(hc):
            @ssec
            def _(s):
                if hc >= 2:
                    s.wait_ge(p.sem("sv"), mk[("hfree", hc - 2)])
                for n in ["gm", "ig", "kk", "vv", "qq", "og"]:
                    p.inc(s.dma_start(out=tl[n][hc % 2][:],
                                      in_=ins[n][hc * 128:(hc + 1) * 128, :]),
                          f"sxt{hc % 2}", 16)

        emit_head_dma(0)
        emit_head_dma(1)

        def fv0(v):
            v.wait_ge(p.sem("sx"), mk["consts"])
            v.memset(onesc[:], 1.0)
        vsec(fv0)

        for hc in range(8):
            bb = hc % 2

            def f1(v, hc=hc, bb=bb):
                v.wait_ge(p.sem(f"sxt{bb}"), 16 * 6 * (hc // 2 + 1))
                v.tensor_tensor(t1[:], tl["ig"][bb][:], tl["kk"][bb][:], OP.mult)
                v.tensor_tensor(t1[:], t1[:], tl["vv"][bb][:], OP.mult)
                v.tensor_tensor_scan(mem[:], tl["gm"][bb][:], t1[:], 0.0,
                                     OP.mult, OP.add)
                mk[("scan", hc)] = p.cnt["sv"]
            vsec(f1)

            for gi in range(4):
                sl = slice(gi * 512, (gi + 1) * 512)

                def fgn(v, a, t, hc=hc, bb=bb, gi=gi, sl=sl, which=0,
                        src=None, wcol=None, bcolw=None):
                    pass

                def g1a(a, sl=sl, hc=hc):
                    a.wait_ge(p.sem("sv"), mk[("scan", hc)])
                    a.activation(sq512[:], mem[:, sl], AF.Square)
                    mk["gsq"] = p.cnt["sa"]
                asec(g1a)

                @tsec
                def _(t, sl=sl, hc=hc):
                    t.wait_ge(p.sem("sa"), mk["gsq"])
                    t.wait_ge(p.sem("sv"), mk.get("gn_stats_read", 0))
                    p.inc(t.matmul(psS[:], onesc[:], mem[:, sl],
                                   start=True, stop=True), "smm")
                    p.inc(t.matmul(psQ[:], onesc[:], sq512[:],
                                   start=True, stop=True), "smm")
                mk["gnmm"] = p.cnt["smm"]

                def g1b(v, sl=sl):
                    v.wait_ge(p.sem("smm"), mk["gnmm"])
                    v.tensor_scalar(sr0[:], psS[0:1, :], 1.0 / 128, None,
                                    OP.mult)
                    v.tensor_tensor(sr1[:], sr0[:], sr0[:],
                                    OP.mult)
                    v.scalar_tensor_tensor(sr2[:], psQ[0:1, :], 1.0 / 128,
                                           sr1[:], OP.mult, OP.subtract)
                    v.tensor_scalar(sr2[:], sr2[:], EPS_GN, None,
                                    OP.add)
                    mk["gnvar"] = p.cnt["sv"]
                vsec(g1b)

                def g1c(a):
                    a.wait_ge(p.sem("sv"), mk["gnvar"])
                    a.activation(sr3[:], sr2[:], AF.Sqrt)
                    mk["gnsd"] = p.cnt["sa"]
                asec(g1c)

                def g1d(v):
                    v.wait_ge(p.sem("sa"), mk["gnsd"])
                    v.reciprocal(sr3[:], sr3[:])
                    mk["gnr"] = p.cnt["sv"]
                vsec(g1d)

                @tsec
                def _(t):
                    t.wait_ge(p.sem("sv"), mk["gnr"])
                    p.inc(t.matmul(psB1[:], c1r[:], sr0[:],
                                   start=True, stop=True), "smm")
                    p.inc(t.matmul(psB2[:], c1r[:], sr3[:],
                                   start=True, stop=True), "smm")
                mk["gnbc"] = p.cnt["smm"]

                def g1e(v, sl=sl, hc=hc, bb=bb):
                    v.wait_ge(p.sem("smm"), mk["gnbc"])
                    v.tensor_tensor(t1[:, sl], mem[:, sl], psB1[:], OP.subtract)
                    v.tensor_tensor(t1[:, sl], t1[:, sl], psB2[:], OP.mult)
                    mk["gn_stats_read"] = p.cnt["sv"]
                    v.tensor_scalar(t1[:, sl], t1[:, sl], cb1[:, hc:hc + 1],
                                    cb2[:, hc:hc + 1], OP.mult, OP.add)
                    v.tensor_tensor(t1[:, sl], t1[:, sl], tl["qq"][bb][:, sl],
                                    OP.mult)
                    mk["gated"] = p.cnt["sv"]
                vsec(g1e)

                def g2a(a, sl=sl):
                    a.wait_ge(p.sem("sv"), mk["gated"])
                    a.activation(sq512[:], t1[:, sl], AF.Square)
                    mk["gsq2"] = p.cnt["sa"]
                asec(g2a)

                @tsec
                def _(t, sl=sl):
                    t.wait_ge(p.sem("sa"), mk["gsq2"])
                    p.inc(t.matmul(psS[:], onesc[:], t1[:, sl],
                                   start=True, stop=True), "smm")
                    p.inc(t.matmul(psQ[:], onesc[:], sq512[:],
                                   start=True, stop=True), "smm")
                mk["gnmm2"] = p.cnt["smm"]

                def g2b(v):
                    v.wait_ge(p.sem("smm"), mk["gnmm2"])
                    v.tensor_scalar(sr0[:], psS[0:1, :], 1.0 / 128, None,
                                    OP.mult)
                    v.tensor_tensor(sr1[:], sr0[:], sr0[:],
                                    OP.mult)
                    v.scalar_tensor_tensor(sr2[:], psQ[0:1, :], 1.0 / 128,
                                           sr1[:], OP.mult, OP.subtract)
                    v.tensor_scalar(sr2[:], sr2[:], EPS_GN, None,
                                    OP.add)
                    mk["gnvar2"] = p.cnt["sv"]
                vsec(g2b)

                def g2c(a):
                    a.wait_ge(p.sem("sv"), mk["gnvar2"])
                    a.activation(sr3[:], sr2[:], AF.Sqrt)
                    mk["gnsd2"] = p.cnt["sa"]
                asec(g2c)

                def g2d(v):
                    v.wait_ge(p.sem("sa"), mk["gnsd2"])
                    v.reciprocal(sr3[:], sr3[:])
                    mk["gnr2"] = p.cnt["sv"]
                vsec(g2d)

                @tsec
                def _(t):
                    t.wait_ge(p.sem("sv"), mk["gnr2"])
                    p.inc(t.matmul(psB1[:], c1r[:], sr0[:],
                                   start=True, stop=True), "smm")
                    p.inc(t.matmul(psB2[:], c1r[:], sr3[:],
                                   start=True, stop=True), "smm")
                mk["gnbc2"] = p.cnt["smm"]

                def g2e(v, sl=sl, hc=hc, bb=bb, gi=gi):
                    v.wait_ge(p.sem("smm"), mk["gnbc2"])
                    v.tensor_tensor(t1[:, sl], t1[:, sl], psB1[:], OP.subtract)
                    v.tensor_tensor(t1[:, sl], t1[:, sl], psB2[:], OP.mult)
                    mk["gn_stats_read"] = p.cnt["sv"]
                    v.tensor_scalar(t1[:, sl], t1[:, sl], cb3[:, hc:hc + 1],
                                    cb4[:, hc:hc + 1], OP.mult, OP.add)
                    v.tensor_tensor(OC[:, hc * T + gi * 512:hc * T + gi * 512 + 512],
                                    t1[:, sl], tl["og"][bb][:, sl], OP.mult)
                    if gi == 3:
                        mk[("hfree", hc)] = p.cnt["sv"]
                    mk[("ocdone", hc)] = p.cnt["sv"]
                vsec(g2e)

            if hc + 2 < 8:
                emit_head_dma(hc + 2)

        mk["oc_all"] = p.cnt["sv"]

        mm_after = {}
        for o in range(16):
            buf = o % 2

            @gsec
            def _(g, o=o, buf=buf):
                if o >= 2:
                    g.wait_ge(p.sem("smm"), mm_after[o - 2])
                srcap = bass.AP(woT8.tensor, o * 128,
                                [[D, 128], [128 * D, 8], [1, 128]])
                dst = bass.AP(wst[buf], 0, [[1024, 128], [128, 8], [1, 128]])
                p.inc(g.dma_start(out=dst, in_=srcap), f"sw{buf}", 16)

            for g4 in range(4):
                pp = psP[g4 % 2]

                @tsec
                def _(t, o=o, buf=buf, g4=g4, pp=pp):
                    if g4 == 0:
                        t.wait_ge(p.sem(f"sw{buf}"), 16 * (o // 2 + 1))
                        if o == 0:
                            t.wait_ge(p.sem("sv"), mk["oc_all"])
                    t.wait_ge(p.sem(f"sev{g4 % 2}"),
                              16 * (2 * o + g4 // 2))
                    for dd in range(8):
                        inst = t.matmul(pp[:], wst[buf][:, dd * 128:(dd + 1) * 128],
                                        OC[:, dd * T + g4 * 512:dd * T + g4 * 512 + 512],
                                        start=(dd == 0), stop=(dd == 7))
                    p.inc(inst, "smm")
                    if g4 == 3:
                        mm_after[o] = p.cnt["smm"]
                mmc = p.cnt["smm"]

                def fev(a, pp=pp, g4=g4, mmc=mmc):
                    a.wait_ge(p.sem("smm"), mmc)
                    a.copy(ev[g4 % 2][:], pp[:])
                    mk["wo_ev"] = p.cnt["sa"]
                asec(fev)

                @ssec
                def _(s, o=o, g4=g4):
                    s.wait_ge(p.sem("sa"), mk["wo_ev"])
                    dst = o_s0p[o * 128:(o + 1) * 128, g4 * 512:(g4 + 1) * 512]
                    p.inc(s.dma_start(out=dst, in_=ev[g4 % 2][:]),
                          f"sev{g4 % 2}", 16)

        @ssec
        def _(s):
            s.wait_ge(p.sem("sev0"), p.cnt["sev0"])
            s.wait_ge(p.sem("sev1"), p.cnt["sev1"])

    return p.nc


def prep_B(ins, aRes):
    maps = []
    mnw = chunk_major_cols(np.asarray(ins["L0_attn_mn_w"], np.float32))
    mnb = chunk_major_cols(np.asarray(ins["L0_attn_mn_b"], np.float32))
    gnw = chunk_major_cols(np.asarray(ins["L0_attn_gn_w"], np.float32))
    gnb = chunk_major_cols(np.asarray(ins["L0_attn_gn_b"], np.float32))
    woT = _cm(np.asarray(ins["L0_attn_Wo"], np.float32).T)
    ones1r = np.ones((1, 128), np.float32)
    for c in range(NCORES):
        b, hs = c // 2, c % 2
        m = {}
        for dn, sn in [("gm", "gmf"), ("ig", "igf"), ("kk", "kf"),
                       ("vv", "vf"), ("qq", "qf"), ("og", "ogf")]:
            full = np.concatenate([aRes[2 * b][sn], aRes[2 * b + 1][sn]], axis=1)
            m[dn] = _cm(full[hs * 1024:(hs + 1) * 1024, :])
        m["mnw"] = _cm(mnw[:, hs * 8:hs * 8 + 8])
        m["mnb"] = _cm(mnb[:, hs * 8:hs * 8 + 8])
        m["gnw"] = _cm(gnw[:, hs * 8:hs * 8 + 8])
        m["gnb"] = _cm(gnb[:, hs * 8:hs * 8 + 8])
        m["woT8"] = _cm(woT[hs * 1024:(hs + 1) * 1024, :])
        m["ones1r"] = ones1r
        maps.append(m)
    return maps


def build_launch_C():
    p = Prog("C")
    nc = p.nc
    xn0r = p.din("xn0r", [R, D])
    x0r = p.din("x0r", [R, D])
    s0pa = p.din("s0pa", [D, R])
    s0pb = p.din("s0pb", [D, R])
    fwT = p.din("fwT", [D, D])
    fb = p.din("fb", [128, 16])
    wsc = p.din("wsc", [1, 1])
    cmax8 = p.din("cmax8", [8, 1])
    identb = p.din("identb", [128, 128], BF16)
    identf = p.din("identf", [128, 128])
    ones1r = p.din("ones1r", [1, 128])

    o_x1f = p.dout("x1f", [D, R])
    o_xn1f = p.dout("xn1f", [D, R])
    o_xn1b = p.dout("xn1b", [D, R], BF16)

    with p.ctx:
        xt = [p.sb(f"xt{i}", [128, D]) for i in range(2)]
        ty = p.sb("ty", [128, D])
        tu = p.sb("tu", [128, D])
        tt = p.sb("tt", [128, D])
        tv = p.sb("tv", [128, D])
        xqb = p.sb("xqb", [128, D], BF16)
        XQF = p.sb("XQF", [128, 16 * 1024], BF16)
        X1F = p.sb("X1F", [128, 16 * 1024])
        wst = [p.sb(f"wst{i}", [128, D], BF16) for i in range(2)]
        s0st = [p.sb(f"s0st{i}", [128, 512]) for i in range(2)]
        ev = [p.sb(f"ev{i}", [128, 512]) for i in range(2)]
        gs = p.sb("gs", [128, 512])
        cid = p.sb("cid", [128, 128], BF16)
        cidf = p.sb("cidf", [128, 128])
        c1r = p.sb("c1r", [1, 128])
        fbs = p.sb("fbs", [128, 16])
        cm8 = p.sb("cm8", [8, 1])
        wss = p.sb("wss", [1, 1])
        g1 = p.sb("g1", [1, 8])
        ccols = p.sb("ccols", [128, 14])
        rscol = p.sb("rscol", [128, 1])
        wscol = p.sb("wscol", [128, 1])
        sq512 = p.sb("sq512", [128, 512])
        sr0 = p.sb("sr0", [1, 512])
        sr1 = p.sb("sr1", [1, 512])
        sr2 = p.sb("sr2", [1, 512])
        onesc = p.sb("onesc", [128, 1])
        xnb = p.sb("xnb", [128, 1024], BF16)
        b0t = p.sb("b0t", [128, D])
        qb16 = p.sb("qb16", [128, D], BF16)
        iqb16 = p.sb("iqb16", [128, D], BF16)

        psP = [p.ps(f"psP{i}", [128, 512]) for i in range(2)]
        psT = [p.ps("psT0", [128, 128], BF16)]
        psTF = [p.ps("psTF0", [128, 128])]
        psS = p.ps("psS", [1, 512])
        psQ = p.ps("psQ", [1, 512])
        psB1 = p.ps("psB1", [128, 512])
        psB2 = p.ps("psB2", [128, 512])

        blk = p.ctx.enter_context(nc.Block())
        for s in ["sx", "sv", "sa", "st", "smm", "sxt0", "sxt1", "sw0", "sw1",
                  "ss00", "ss01", "sev0", "sev1", "so", "sx1"]:
            p.sem(s)

        def vsec(f):
            @blk.vector
            def _(v):
                f(p.chain(v, "sv"))

        def asec(f):
            @blk.scalar
            def _(a):
                f(p.chain(a, "sa"))

        tsec, ssec, gsec = blk.tensor, blk.sync, blk.gpsimd
        mk = {}

        @ssec
        def _(s):
            for dst, srcap in [(cid[:], identb), (cidf[:], identf),
                               (c1r[:], ones1r), (fbs[:], fb),
                               (cm8[:], cmax8), (wss[:], wsc)]:
                p.inc(s.dma_start(out=dst, in_=srcap), "sx", 16)
            mk["consts"] = p.cnt["sx"]

        def emit_xt(j, which):
            @ssec
            def _(s):
                if ("xtfree", j - 2) in mk:
                    s.wait_ge(p.sem("sv"), mk[("xtfree", j - 2)])
                p.inc(s.dma_start(out=xt[j % 2][:],
                                  in_=which[j * 128:(j + 1) * 128, :]),
                      f"sxt{j % 2}", 16)

        def f0(v):
            v.wait_ge(p.sem("sx"), mk["consts"])
            v.memset(onesc[:], 1.0)
            for i, val in enumerate([16.0, 8.0, 4.0, 2.0, 1.0, INV_LN2, 16.0,
                                     LN2, -19.0 * LN2, -LN2, 19.0 * LN2, 0.0,
                                     32.0]):
                v.memset(ccols[:, i:i + 1], val)
            mk["cc"] = p.cnt["sv"]
        vsec(f0)

        @tsec
        def _(t):
            t.wait_ge(p.sem("sv"), mk["cc"])
            p.inc(t.matmul(psS[0:1, 0:8], cm8[:], cidf[0:8, 0:8],
                           start=True, stop=True), "smm")
        mk["g8"] = p.cnt["smm"]

        def f1(v):
            v.wait_ge(p.sem("smm"), mk["g8"])
            v.tensor_copy(g1[:], psS[0:1, 0:8])
            v.tensor_reduce(g1[0:1, 0:1], g1[:], AX, OP.max)
            v.tensor_scalar(g1[0:1, 0:1], g1[0:1, 0:1], 1.0 / 448.0, None, OP.mult)
            v.tensor_scalar(g1[0:1, 0:1], g1[0:1, 0:1], 1e-7, None, OP.max)
            v.reciprocal(g1[0:1, 1:2], g1[0:1, 0:1])
            mk["rs"] = p.cnt["sv"]
        vsec(f1)

        @tsec
        def _(t):
            t.wait_ge(p.sem("sv"), mk["rs"])
            p.inc(t.matmul(psB1[0:128, 0:2], c1r[:], g1[0:1, 0:2],
                           start=True, stop=True), "smm")
        mk["rsb"] = p.cnt["smm"]

        def f2(v):
            v.wait_ge(p.sem("smm"), mk["rsb"])
            v.tensor_copy(rscol[:], psB1[0:128, 1:2])
            v.tensor_copy(wscol[:], psB1[0:128, 0:1])
            mk["rscol"] = p.cnt["sv"]
        vsec(f2)

        @tsec
        def _(t):
            t.wait_ge(p.sem("sv"), mk["rscol"])
            p.inc(t.matmul(psB2[0:128, 0:1], c1r[:], wss[:],
                           start=True, stop=True), "smm")
        mk["wsb"] = p.cnt["smm"]

        def f3(v):
            v.wait_ge(p.sem("smm"), mk["wsb"])
            v.tensor_copy(wscol[:], psB2[0:128, 0:1])
            mk["wscol"] = p.cnt["sv"]
        vsec(f3)

        def ladder(v, u, scr, bits, save_b0=None):
            for bit in bits:
                i = {16.0: 0, 8.0: 1, 4.0: 2, 2.0: 3, 1.0: 4, 32.0: 12}[bit]
                v.tensor_scalar(scr, u, ccols[:, i:i + 1], ccols[:, i:i + 1],
                                OP.is_ge, OP.mult)
                if bit == 1.0 and save_b0 is not None:
                    v.tensor_copy(save_b0, scr)
                v.tensor_sub(u, u, scr)

        emit_xt(0, xn0r)
        emit_xt(1, xn0r)
        for j in range(NT):
            def fq(v, j=j):
                v.wait_ge(p.sem(f"sxt{j % 2}"), 16 * (j // 2 + 1))
                v.wait_ge(p.sem("st"), p.cnt["st"])
                src = xt[j % 2]
                v.tensor_scalar_mul(ty[:], src[:], rscol[:])
                mk["y"] = p.cnt["sv"]
            vsec(fq)

            def fa(a, j=j):
                a.wait_ge(p.sem("sv"), mk["y"])
                a.activation(tu[:], ty[:], AF.Abs)
                a.activation(tu[:], tu[:], AF.Ln)
                mk["lg"] = p.cnt["sa"]
            asec(fa)

            def fq2(v, j=j):
                v.wait_ge(p.sem("sa"), mk["lg"])
                v.tensor_scalar(tu[:], tu[:], ccols[:, 5:6], ccols[:, 6:7],
                                OP.mult, OP.add)
                v.tensor_copy(tv[:], tu[:])
                ladder(v, tv[:], tt[:], [16.0, 8.0, 4.0, 2.0, 1.0])
                v.tensor_sub(tu[:], tu[:], tv[:])
                v.tensor_scalar(tu[:], tu[:], 10.0, None, OP.max)
                mk["ef"] = p.cnt["sv"]
            vsec(fq2)

            def fb2(a, j=j):
                a.wait_ge(p.sem("sv"), mk["ef"])
                a.activation(qb16[:], tu[:], AF.Exp, scale=ccols[:, 7:8],
                             bias=ccols[:, 8:9])
                a.activation(iqb16[:], tu[:], AF.Exp, scale=ccols[:, 9:10],
                             bias=ccols[:, 10:11])
                mk["qq"] = p.cnt["sa"]
            asec(fb2)

            def fq3(v, j=j):
                v.wait_ge(p.sem("sa"), mk["qq"])
                v.tensor_tensor(tv[:], ty[:], iqb16[:], OP.mult)
                v.tensor_scalar(tu[:], tv[:], 16.5, None, OP.add)
                ladder(v, tu[:], ty[:], [32.0, 16.0, 8.0, 4.0, 2.0, 1.0],
                       save_b0=b0t[:])
                v.tensor_scalar(ty[:], tv[:], 16.5, None, OP.add)
                v.tensor_sub(tv[:], ty[:], tu[:])
                v.tensor_scalar(tv[:], tv[:], -16.0, None, OP.add)
                v.tensor_scalar(tu[:], tu[:], ccols[:, 11:12], None, OP.is_equal)
                v.tensor_tensor(tu[:], tu[:], b0t[:], OP.mult)
                v.tensor_sub(tv[:], tv[:], tu[:])
                v.tensor_tensor(xqb[:], tv[:], qb16[:], OP.mult)
                mk[("xtfree", j)] = p.cnt["sv"]
                mk["xq"] = p.cnt["sv"]
            vsec(fq3)

            if j + 2 < NT:
                emit_xt(j + 2, xn0r)

            vr = mk["xq"]
            evb = p.cnt["sa"]

            @tsec
            def _(t, j=j, vr=vr, evb=evb):
                t.wait_ge(p.sem("sv"), vr)
                for cch in range(16):
                    t.wait_ge(p.sem("sa"), evb + cch if cch >= 1 else evb)
                    p.inc(t.transpose(psT[0][:],
                                      xqb[:, cch * 128:(cch + 1) * 128], cid[:]),
                          "st")
            stb = p.cnt["st"] - 16
            for cch in range(16):
                def fev(a, cch=cch, j=j, stb=stb):
                    a.wait_ge(p.sem("st"), stb + cch + 1)
                    a.copy(XQF[:, cch * 1024 + j * 128: cch * 1024 + j * 128 + 128],
                           psT[0][:])
                asec(fev)

        mk["xqf_all"] = p.cnt["sa"]

        mm_after = {}
        for o in range(16):
            buf = o % 2

            @gsec
            def _(g, o=o, buf=buf):
                if o >= 2:
                    g.wait_ge(p.sem("smm"), mm_after[o - 2])
                srcap = bass.AP(fwT.tensor, o * 128,
                                [[D, 128], [128 * D, 16], [1, 128]])
                dst = bass.AP(wst[buf], 0, [[D, 128], [128, 16], [1, 128]])
                p.inc(g.dma_start(out=dst, in_=srcap), f"sw{buf}", 16)

            for g2 in range(2):
                pp = psP[g2]
                sb2 = s0st[g2]

                @ssec
                def _(s, o=o, g2=g2, sb2=sb2):
                    s.wait_ge(p.sem("sv"), mk.get(("s0free", o - 1, g2), 0))
                    sl_r = slice(o * 128, (o + 1) * 128)
                    sl_c = slice(g2 * 512, (g2 + 1) * 512)
                    p.inc(s.dma_start(out=sb2[:], in_=s0pa[sl_r, sl_c]),
                          f"ss0{g2}", 16)
                    p.inc(s.dma_start(out=ev[g2][:], in_=s0pb[sl_r, sl_c]),
                          f"sev{g2}", 16)

                @tsec
                def _(t, o=o, buf=buf, g2=g2, pp=pp):
                    if g2 == 0:
                        t.wait_ge(p.sem(f"sw{buf}"), 16 * (o // 2 + 1))
                        if o == 0:
                            t.wait_ge(p.sem("sa"), mk["xqf_all"])
                    if ("pfree", o - 1, g2) in mk:
                        t.wait_ge(p.sem("sv"), mk[("pfree", o - 1, g2)])
                    for dd in range(16):
                        inst = t.matmul(pp[:], wst[buf][:, dd * 128:(dd + 1) * 128],
                                        XQF[:, dd * 1024 + g2 * 512:
                                            dd * 1024 + g2 * 512 + 512],
                                        start=(dd == 0), stop=(dd == 15))
                    p.inc(inst, "smm")
                    if g2 == 1:
                        mm_after[o] = p.cnt["smm"]
                mmc = p.cnt["smm"]

                def fsil(a, pp=pp, o=o, mmc=mmc):
                    a.wait_ge(p.sem("smm"), mmc)
                    a.wait_ge(p.sem("sv"), mk.get("pfree_last", 0))
                    a.activation(gs[:], pp[:], AF.Silu, scale=wscol[:],
                                 bias=fbs[:, o:o + 1])
                    mk["gsil"] = p.cnt["sa"]
                asec(fsil)

                def fres(v, o=o, g2=g2, sb2=sb2):
                    v.wait_ge(p.sem("sa"), mk["gsil"])
                    v.wait_ge(p.sem(f"ss0{g2}"), 16 * (o + 1))
                    v.wait_ge(p.sem(f"sev{g2}"), 16 * (o + 1))
                    dst = X1F[:, o * 1024 + g2 * 512: o * 1024 + g2 * 512 + 512]
                    v.tensor_tensor(dst, sb2[:], ev[g2][:], OP.add)
                    mk[("s0free", o, g2)] = p.cnt["sv"]
                    v.tensor_tensor(dst, dst, gs[:], OP.mult)
                    mk[("pfree", o, g2)] = p.cnt["sv"]
                    mk["pfree_last"] = p.cnt["sv"]
                vsec(fres)

        def emit_x0_dma(j):
            @ssec
            def _(s, j=j):
                if ("x0free", j - 2) in mk:
                    s.wait_ge(p.sem("st"), mk[("x0free", j - 2)])
                else:
                    s.wait_ge(p.sem("sv"), mk[("xtfree", NT - 2 + j % 2)])
                p.inc(s.dma_start(out=xt[j % 2][:],
                                  in_=x0r[j * 128:(j + 1) * 128, :]),
                      f"sxt{j % 2}", 16)
                mk[("x0dma", j)] = p.cnt[f"sxt{j % 2}"]

        emit_x0_dma(0)
        emit_x0_dma(1)
        for j in range(NT):
            evb = p.cnt["sv"]

            @tsec
            def _(t, j=j, evb=evb):
                t.wait_ge(p.sem(f"sxt{j % 2}"), mk[("x0dma", j)])
                if j == 0:
                    t.wait_ge(p.sem("sv"), mk[("pfree", 15, 1)])
                for cch in range(16):
                    t.wait_ge(p.sem("sv"), evb + cch if cch >= 1 else evb)
                    inst = t.transpose(psTF[0][:],
                                       xt[j % 2][:, cch * 128:(cch + 1) * 128],
                                       cidf[:])
                    p.inc(inst, "st")
                    if cch == 15:
                        mk[("x0free", j)] = p.cnt["st"]
            stb = p.cnt["st"] - 16
            for cch in range(16):
                def fadd(v, cch=cch, j=j, stb=stb):
                    v.wait_ge(p.sem("st"), stb + cch + 1)
                    dst = X1F[:, cch * 1024 + j * 128: cch * 1024 + j * 128 + 128]
                    v.tensor_tensor(dst, dst, psTF[0][:], OP.add)
                vsec(fadd)
            if j + 2 < NT:
                emit_x0_dma(j + 2)

        mk["x1_all"] = p.cnt["sv"]

        @ssec
        def _(s):
            s.wait_ge(p.sem("sv"), mk["x1_all"])
            dstap = bass.AP(o_x1f.tensor, 0,
                            [[1024, 128], [128 * 1024, 16], [1, 1024]])
            p.inc(s.dma_start(out=dstap, in_=X1F[:]), "sx1", 16)

        for g2 in range(2):
            sl = slice(g2 * 512, (g2 + 1) * 512)

            def fsq(a, g2=g2, sl=sl):
                a.wait_ge(p.sem("sv"), mk["x1_all"])
                for cch in range(16):
                    a.activation(sq512[:], X1F[:, cch * 1024 + g2 * 512:
                                               cch * 1024 + g2 * 512 + 512],
                                 AF.Square)
                    mk[("lsq", cch)] = p.cnt["sa"]
            for cch in range(16):
                def fsq1(a, g2=g2, sl=sl, cch=cch):
                    a.wait_ge(p.sem("sv"), mk["x1_all"])
                    if ("lmm", g2, cch - 1) in mk:
                        a.wait_ge(p.sem("smm"), mk[("lmm", g2, cch - 1)])
                    a.activation(sq512[:], X1F[:, cch * 1024 + g2 * 512:
                                               cch * 1024 + g2 * 512 + 512],
                                 AF.Square)
                    mk[("lsq", g2, cch)] = p.cnt["sa"]
                asec(fsq1)

                @tsec
                def _(t, g2=g2, cch=cch):
                    t.wait_ge(p.sem("sa"), mk[("lsq", g2, cch)])
                    if cch == 0:
                        t.wait_ge(p.sem("sv"), mk.get(("lnstat_read", g2 - 1), 0))
                    p.inc(t.matmul(psS[:], onesc[:],
                                   X1F[:, cch * 1024 + g2 * 512:
                                       cch * 1024 + g2 * 512 + 512],
                                   start=(cch == 0), stop=(cch == 15)), "smm")
                    p.inc(t.matmul(psQ[:], onesc[:], sq512[:],
                                   start=(cch == 0), stop=(cch == 15)), "smm")
                    mk[("lmm", g2, cch)] = p.cnt["smm"]

            def fst(v, g2=g2):
                v.wait_ge(p.sem("smm"), mk[("lmm", g2, 15)])
                v.tensor_scalar(sr0[:], psS[0:1, :], 1.0 / D, None, OP.mult)
                v.tensor_tensor(sr1[:], sr0[:], sr0[:], OP.mult)
                v.scalar_tensor_tensor(sr2[:], psQ[0:1, :], 1.0 / D, sr1[:],
                                       OP.mult, OP.subtract)
                v.tensor_scalar(sr2[:], sr2[:], EPS_LN, None, OP.add)
                mk[("lnstat_read", g2)] = p.cnt["sv"]
                mk["lvar"] = p.cnt["sv"]
            vsec(fst)

            def fsd(a, g2=g2):
                a.wait_ge(p.sem("sv"), mk["lvar"])
                a.activation(sr2[:], sr2[:], AF.Sqrt)
                mk["lsd"] = p.cnt["sa"]
            asec(fsd)

            def frc(v, g2=g2):
                v.wait_ge(p.sem("sa"), mk["lsd"])
                v.reciprocal(sr2[:], sr2[:])
                mk["lrs"] = p.cnt["sv"]
            vsec(frc)

            @tsec
            def _(t, g2=g2):
                t.wait_ge(p.sem("sv"), mk["lrs"])
                p.inc(t.matmul(psB1[:], c1r[:], sr0[:], start=True, stop=True),
                      "smm")
                p.inc(t.matmul(psB2[:], c1r[:], sr2[:], start=True, stop=True),
                      "smm")
            mk[("lbc", g2)] = p.cnt["smm"]

            for cch in range(16):
                def fnm(v, g2=g2, cch=cch):
                    v.wait_ge(p.sem("smm"), mk[("lbc", g2)])
                    v.wait_ge(p.sem("so"), mk.get("xn1st_last", 0))
                    src = X1F[:, cch * 1024 + g2 * 512: cch * 1024 + g2 * 512 + 512]
                    v.tensor_tensor(gs[:], src, psB1[:], OP.subtract)
                    v.tensor_tensor(gs[:], gs[:], psB2[:], OP.mult)
                    v.tensor_copy(xnb[:, 0:512], gs[:])
                    mk[("xn1", g2, cch)] = p.cnt["sv"]
                vsec(fnm)

                @ssec
                def _(s, g2=g2, cch=cch):
                    s.wait_ge(p.sem("sv"), mk[("xn1", g2, cch)])
                    sl_r = slice(cch * 128, (cch + 1) * 128)
                    sl_c = slice(g2 * 512, (g2 + 1) * 512)
                    p.inc(s.dma_start(out=o_xn1f[sl_r, sl_c], in_=gs[:]), "so", 16)
                    p.inc(s.dma_start(out=o_xn1b[sl_r, sl_c], in_=xnb[:, 0:512]),
                          "so", 16)
                    mk["xn1st_last"] = p.cnt["so"]

        @ssec
        def _(s):
            s.wait_ge(p.sem("so"), p.cnt["so"])
            s.wait_ge(p.sem("sx1"), p.cnt["sx1"])

    return p.nc


def prep_C(ins, aRes, bRes):
    fwT = _cm(np.asarray(ins["L0_ffn_weight"], np.float32).T)
    fb = chunk_major_cols(np.asarray(ins["L0_ffn_bias"], np.float32))
    wsc = np.asarray(ins["L0_ffn_weight_scale"], np.float32).reshape(1, 1)
    cmax8 = np.stack([aRes[c]["cmax"].ravel() for c in range(8)]).reshape(8, 1)
    import ml_dtypes
    identb = np.eye(128, dtype=np.float32).astype(ml_dtypes.bfloat16)
    identf = np.eye(128, dtype=np.float32)
    ones1r = np.ones((1, 128), np.float32)
    x = np.asarray(ins["L0_attn_x"], np.float32).reshape(B * T, D)
    maps = []
    for c in range(NCORES):
        b, half = c // 2, c % 2
        sl = slice(half * 1024, (half + 1) * 1024)
        m = {
            "xn0r": _cm(aRes[c]["xn0"]),
            "x0r": _cm(x[c * R:(c + 1) * R, :]),
            "s0pa": _cm(bRes[2 * b]["s0p"][:, sl]),
            "s0pb": _cm(bRes[2 * b + 1]["s0p"][:, sl]),
            "fwT": fwT, "fb": fb, "wsc": wsc, "cmax8": cmax8,
            "identb": identb, "identf": identf, "ones1r": ones1r,
        }
        maps.append(m)
    return maps


def build_launch_D():
    p = Prog("D")
    nc = p.nc
    xn1b = p.din("xn1b", [D, R], BF16)
    wqi = p.din("wqi", [D, D])
    wq = p.din("wq", [D, D])
    wki = p.din("wki", [D, 128])
    wk = p.din("wk", [D, 128])
    wv = p.din("wv", [D, 128])
    wiw = p.din("wiw", [D, 16])
    w1Ts = p.din("w1Ts", [D, 1024])
    w2Ts = p.din("w2Ts", [1024, INNER // 8 * 0 + D])
    hadb = p.din("hadb", [128, 128], BF16)
    identb = p.din("identb", [128, 128], BF16)
    identf = p.din("identf", [128, 128])

    o_qih = p.dout("qih", [D, R], BF16)
    o_qb = p.dout("qb", [D, R], BF16)
    o_kih = p.dout("kih", [128, R], BF16)
    o_kvt = p.dout("kvt", [R, 256], BF16)
    o_kf = p.dout("kf", [128, R], BF16)
    o_iwt = p.dout("iwt", [R, 16])
    o_sgn1 = p.dout("sgn1", [D, 1024], BF16)
    o_sgn2 = p.dout("sgn2", [1024, D], BF16)
    o_w1s = p.dout("w1s", [1, 1])
    o_w2s = p.dout("w2s", [1, 1])

    with p.ctx:
        XB = p.sb("XB", [128, 16 * 1024], BF16)
        wst = [p.sb(f"wst{i}", [128, D], BF16) for i in range(2)]
        hb = p.sb("hb", [128, 128], BF16)
        cid = p.sb("cid", [128, 128], BF16)
        cidf = p.sb("cidf", [128, 128])
        evb = [p.sb(f"evb{i}", [128, 512], BF16) for i in range(2)]
        kb = p.sb("kb", [128, 1024], BF16)
        vb = p.sb("vb", [128, 1024], BF16)
        kvrow = p.sb("kvrow", [128, 256], BF16)
        iwrow = p.sb("iwrow", [16, 1024])
        iwc = p.sb("iwc", [128, 16])
        wbig = p.sb("wbig", [128, 16 * 1024])
        sgnb = p.sb("sgnb", [128, 16 * 1024], BF16)
        acol = p.sb("acol", [128, 1])
        bcol = p.sb("bcol", [128, 1])
        r1 = p.sb("r1", [1, 128])

        psP = [p.ps(f"psP{i}", [128, 512]) for i in range(2)]
        psH = p.ps("psH", [128, 512])
        psT = p.ps("psT", [128, 128], BF16)
        psTF = p.ps("psTF", [128, 16])
        psX = p.ps("psX", [1, 128])

        blk = p.ctx.enter_context(nc.Block())
        for s in ["sx", "sv", "sa", "sg", "st", "smm", "sw0", "sw1", "so",
                  "sxb", "sws", "sst0", "sst1", "skv", "siw", "ssg"]:
            p.sem(s)

        def vsec(f):
            @blk.vector
            def _(v):
                f(p.chain(v, "sv"))

        def asec(f):
            @blk.scalar
            def _(a):
                f(p.chain(a, "sa"))

        def gsec_ch(f):
            @blk.gpsimd
            def _(g):
                f(p.chain(g, "sg"))

        tsec, ssec, gsec = blk.tensor, blk.sync, blk.gpsimd
        mk = {}

        @ssec
        def _(s):
            for dst, srcap in [(hb[:], hadb), (cid[:], identb), (cidf[:], identf)]:
                p.inc(s.dma_start(out=dst, in_=srcap), "sx", 16)
            mk["consts"] = p.cnt["sx"]
            p.inc(s.dma_start(out=XB[:],
                              in_=bass.AP(xn1b.tensor, 0,
                                          [[1024, 128], [128 * 1024, 16],
                                           [1, 1024]])), "sxb", 16)

        def proj(wname, wap, ncol, had, store_cb, wi_base, ow=128):
            for o in range(ncol):
                buf = (wi_base + o) % 2

                @gsec
                def _(g, o=o, buf=buf, wap=wap, wi=wi_base + o, ow=ow):
                    if wi >= 2:
                        g.wait_ge(p.sem("smm"), mk.get(("mmafter", wi - 2), 0))
                    srcap = bass.AP(wap.tensor, o * ow,
                                    [[wap.shape[1], 128],
                                     [128 * wap.shape[1], 16], [1, ow]])
                    dst = bass.AP(wst[buf], 0, [[D, 128], [ow, 16], [1, ow]])
                    p.inc(g.dma_start(out=dst, in_=srcap), f"sw{buf}", 16)

                for g2 in range(2):
                    pp = psP[g2]

                    @tsec
                    def _(t, o=o, buf=buf, g2=g2, pp=pp, wi=wi_base + o, ow=ow):
                        if g2 == 0:
                            t.wait_ge(p.sem(f"sw{buf}"), 16 * (wi // 2 + 1))
                            if wi == 0:
                                t.wait_ge(p.sem("sxb"), 16)
                                t.wait_ge(p.sem("sx"), mk["consts"])
                        t.wait_ge(p.sem(f"sst{g2}"), mk.get(("evst", g2), 0))
                        t.wait_ge(p.sem("sa"), mk.get(("ppdone", g2), 0))
                        for dd in range(16):
                            inst = t.matmul(pp[0:ow, :],
                                            wst[buf][:, dd * ow:(dd + 1) * ow],
                                            XB[:, dd * 1024 + g2 * 512:
                                               dd * 1024 + g2 * 512 + 512],
                                            start=(dd == 0), stop=(dd == 15))
                        p.inc(inst, "smm")
                        if g2 == 1:
                            mk[("mmafter", wi)] = p.cnt["smm"]
                    store_cb(o, g2, pp, p.cnt["smm"])

        def mk_simple(oname, dt_out, nrow=128):
            def cb(o, g2, pp, mmc):
                def fe(a, o=o, g2=g2, pp=pp, mmc=mmc):
                    a.wait_ge(p.sem("smm"), mmc)
                    a.copy(evb[g2][0:nrow, :], pp[0:nrow, :])
                    mk[("evrdy", g2)] = p.cnt["sa"]
                asec(fe)

                @ssec
                def _(s, o=o, g2=g2):
                    s.wait_ge(p.sem("sa"), mk[("evrdy", g2)])
                    dst = oname[o * 128:(o + 1) * 128, g2 * 512:(g2 + 1) * 512]
                    p.inc(s.dma_start(out=dst, in_=evb[g2][:]), "so", 16)
                    mk[("evfree", g2)] = 0
            return cb

        def cb_qih(o, g2, pp, mmc):
            def fe(a, o=o, g2=g2, pp=pp, mmc=mmc):
                a.wait_ge(p.sem("smm"), mmc)
                a.wait_ge(p.sem("st"), mk.get("hH", 0))
                a.copy(evb[g2][:], pp[:])
                mk["qi_ev"] = p.cnt["sa"]
                mk[("ppdone", g2)] = p.cnt["sa"]
            asec(fe)

            @tsec
            def _(t, o=o, g2=g2):
                t.wait_ge(p.sem("sa"), mk["qi_ev"])
                t.wait_ge(p.sem("sa"), mk.get("hH_ev", 0))
                p.inc(t.matmul(psH[:], hb[:], evb[g2][:], start=True, stop=True),
                      "st")
                mk["hH"] = p.cnt["st"]

            def fe2(a, o=o, g2=g2):
                a.wait_ge(p.sem("st"), mk["hH"])
                a.copy(evb[g2][:], psH[:])
                mk["hH_ev"] = p.cnt["sa"]
            asec(fe2)

            @ssec
            def _(s, o=o, g2=g2):
                s.wait_ge(p.sem("sa"), mk["hH_ev"])
                dst = o_qih[o * 128:(o + 1) * 128, g2 * 512:(g2 + 1) * 512]
                p.inc(s.dma_start(out=dst, in_=evb[g2][:]), f"sst{g2}", 16)
                mk[("evst", g2)] = p.cnt[f"sst{g2}"]
        proj("wqi", wqi, 16, True, cb_qih, 0)

        def cb_qb(o, g2, pp, mmc):
            def fe(a, o=o, g2=g2, pp=pp, mmc=mmc):
                a.wait_ge(p.sem("smm"), mmc)
                a.copy(evb[g2][:], pp[:])
                mk["q_ev"] = p.cnt["sa"]
                mk[("ppdone", g2)] = p.cnt["sa"]
            asec(fe)

            @ssec
            def _(s, o=o, g2=g2):
                s.wait_ge(p.sem("sa"), mk["q_ev"])
                dst = o_qb[o * 128:(o + 1) * 128, g2 * 512:(g2 + 1) * 512]
                p.inc(s.dma_start(out=dst, in_=evb[g2][:]), f"sst{g2}", 16)
                mk[("evst", g2)] = p.cnt[f"sst{g2}"]
        proj("wq", wq, 16, False, cb_qb, 16)

        def cb_kih(o, g2, pp, mmc):
            def fe(a, g2=g2, pp=pp, mmc=mmc):
                a.wait_ge(p.sem("smm"), mmc)
                a.wait_ge(p.sem("st"), mk.get("kiH", 0))
                a.copy(evb[g2][:], pp[:])
                mk["ki_ev"] = p.cnt["sa"]
                mk[("ppdone", g2)] = p.cnt["sa"]
            asec(fe)

            @tsec
            def _(t, g2=g2):
                t.wait_ge(p.sem("sa"), mk["ki_ev"])
                t.wait_ge(p.sem("sa"), mk.get("kiH_ev", 0))
                p.inc(t.matmul(psH[:], hb[:], evb[g2][:], start=True, stop=True),
                      "st")
                mk["kiH"] = p.cnt["st"]

            def fe2(a, g2=g2):
                a.wait_ge(p.sem("st"), mk["kiH"])
                a.copy(evb[g2][:], psH[:])
                mk["kiH_ev"] = p.cnt["sa"]
            asec(fe2)

            @ssec
            def _(s, g2=g2):
                s.wait_ge(p.sem("sa"), mk["kiH_ev"])
                p.inc(s.dma_start(out=o_kih[:, g2 * 512:(g2 + 1) * 512],
                                  in_=evb[g2][:]), f"sst{g2}", 16)
                mk[("evst", g2)] = p.cnt[f"sst{g2}"]
        proj("wki", wki, 1, True, cb_kih, 32)

        def cb_k(o, g2, pp, mmc):
            def fe(a, g2=g2, pp=pp, mmc=mmc):
                a.wait_ge(p.sem("smm"), mmc)
                a.copy(kb[:, g2 * 512:(g2 + 1) * 512], pp[:])
                mk["k_ev"] = p.cnt["sa"]
                mk[("ppdone", g2)] = p.cnt["sa"]
            asec(fe)
        proj("wk", wk, 1, False, cb_k, 33)

        def cb_v(o, g2, pp, mmc):
            def fe(a, g2=g2, pp=pp, mmc=mmc):
                a.wait_ge(p.sem("smm"), mmc)
                a.copy(vb[:, g2 * 512:(g2 + 1) * 512], pp[:])
                mk["v_ev"] = p.cnt["sa"]
                mk[("ppdone", g2)] = p.cnt["sa"]
            asec(fe)
        proj("wv", wv, 1, False, cb_v, 34)

        def cb_iw(o, g2, pp, mmc):
            def fe(a, g2=g2, pp=pp, mmc=mmc):
                a.wait_ge(p.sem("smm"), mmc)
                a.copy(iwrow[:, g2 * 512:(g2 + 1) * 512], pp[0:16, :])
                mk["iw_ev"] = p.cnt["sa"]
                mk[("ppdone", g2)] = p.cnt["sa"]
            asec(fe)
        proj("wiw", wiw, 1, False, cb_iw, 35, ow=16)

        def fiw(v):
            v.wait_ge(p.sem("sa"), mk["iw_ev"])
            v.tensor_scalar(iwrow[:], iwrow[:], 0.0078125, None, OP.mult)
            mk["iw_s"] = p.cnt["sv"]
        vsec(fiw)

        for j in range(NT):
            @tsec
            def _(t, j=j):
                t.wait_ge(p.sem("sv"), mk["iw_s"])
                t.wait_ge(p.sem("sa"), mk.get("iw_t_ev", 0))
                p.inc(t.transpose(psTF[:], iwrow[0:16, j * 128:(j + 1) * 128],
                                  cidf[0:16, 0:16]), "st")
                mk["iw_t"] = p.cnt["st"]

            def fe(a, j=j):
                a.wait_ge(p.sem("st"), mk["iw_t"])
                a.wait_ge(p.sem("siw"), mk.get(("iwst", j - 1), 0))
                a.copy(iwc[:], psTF[:])
                mk["iw_t_ev"] = p.cnt["sa"]
            asec(fe)

            @ssec
            def _(s, j=j):
                s.wait_ge(p.sem("sa"), mk["iw_t_ev"])
                p.inc(s.dma_start(out=o_iwt[j * 128:(j + 1) * 128, :], in_=iwc[:]),
                      "siw", 16)
                mk[("iwst", j)] = p.cnt["siw"]

        for j in range(NT):
            @tsec
            def _(t, j=j):
                t.wait_ge(p.sem("sa"), mk["k_ev"])
                t.wait_ge(p.sem("sa"), mk["v_ev"])
                t.wait_ge(p.sem("sa"), mk.get("kv_ev", 0))
                p.inc(t.transpose(psT[:], kb[:, j * 128:(j + 1) * 128], cid[:]),
                      "st")
                mk["kT"] = p.cnt["st"]

            def fek(a, j=j):
                a.wait_ge(p.sem("st"), mk["kT"])
                a.wait_ge(p.sem("skv"), mk.get(("kvst", j - 1), 0))
                a.copy(kvrow[:, 0:128], psT[:])
                mk["kv_ev"] = p.cnt["sa"]
            asec(fek)

            @tsec
            def _(t, j=j):
                t.wait_ge(p.sem("sa"), mk["kv_ev"])
                p.inc(t.transpose(psT[:], vb[:, j * 128:(j + 1) * 128], cid[:]),
                      "st")
                mk["vT"] = p.cnt["st"]

            def fev(a, j=j):
                a.wait_ge(p.sem("st"), mk["vT"])
                a.copy(kvrow[:, 128:256], psT[:])
                mk["kv_ev"] = p.cnt["sa"]
            asec(fev)

            @ssec
            def _(s, j=j):
                s.wait_ge(p.sem("sa"), mk["kv_ev"])
                p.inc(s.dma_start(out=o_kvt[j * 128:(j + 1) * 128, :],
                                  in_=kvrow[:]), "skv", 16)
                mk[("kvst", j)] = p.cnt["skv"]

        def signprep(wap, oname, osum, nchunk, width, mark, prev=None):
            @ssec
            def _(s):
                if prev is not None:
                    s.wait_ge(p.sem("sa"), mk[(prev, "sgn")])
                    s.wait_ge(p.sem("sv"), mk[(prev, "as")])
                    s.wait_ge(p.sem("ssg"), mk[(prev, "stored")])
                p.inc(s.dma_start(out=wbig[:, 0:nchunk * width],
                                  in_=bass.AP(wap.tensor, 0,
                                              [[width, 128],
                                               [128 * width, nchunk],
                                               [1, width]])), "sws", 16)
                mk[(mark, "ld")] = p.cnt["sws"]

            def fs(a):
                a.wait_ge(p.sem("sws"), mk[(mark, "ld")])
                for cch in range(nchunk):
                    a.activation(sgnb[:, cch * width:(cch + 1) * width],
                                 wbig[:, cch * width:(cch + 1) * width], AF.Sign)
                mk[(mark, "sgn")] = p.cnt["sa"]
            asec(fs)

            def fr(v):
                v.wait_ge(p.sem("sws"), mk[(mark, "ld")])
                v.reduce_sum(acol[:], wbig[:, 0:width], AX,
                             apply_absolute_value=True)
                for cch in range(1, nchunk):
                    v.reduce_sum(bcol[:], wbig[:, cch * width:(cch + 1) * width],
                                 AX, apply_absolute_value=True)
                    v.tensor_add(acol[:], acol[:], bcol[:])
                mk[(mark, "as")] = p.cnt["sv"]
            vsec(fr)

            @tsec
            def _(t):
                t.wait_ge(p.sem("sv"), mk[(mark, "as")])
                t.wait_ge(p.sem("sv"), mk.get("psX_rd", 0))
                p.inc(t.matmul(psX[:], acol[:], cidf[:], start=True, stop=True),
                      "smm")
                mk[(mark, "mm")] = p.cnt["smm"]

            def f2(v):
                v.wait_ge(p.sem("smm"), mk[(mark, "mm")])
                v.wait_ge(p.sem("so"), p.cnt["so"])
                v.tensor_copy(r1[:], psX[:])
                v.tensor_reduce(r1[0:1, 0:1], r1[:], AX, OP.add)
                mk[(mark, "sum")] = p.cnt["sv"]
                mk["psX_rd"] = p.cnt["sv"]
            vsec(f2)

            @ssec
            def _(s):
                s.wait_ge(p.sem("sv"), mk[(mark, "sum")])
                p.inc(s.dma_start(out=osum, in_=r1[0:1, 0:1]), "so", 16)

            for cch in range(nchunk):
                @ssec
                def _(s, cch=cch):
                    s.wait_ge(p.sem("sa"), mk[(mark, "sgn")])
                    p.inc(s.dma_start(out=oname[cch * 128:(cch + 1) * 128, :],
                                      in_=sgnb[:, cch * width:(cch + 1) * width]),
                          "ssg", 16)
            mk[(mark, "stored")] = p.cnt["ssg"]

        @ssec
        def _(s):
            s.wait_ge(p.sem("sa"), mk["k_ev"])
            p.inc(s.dma_start(out=o_kf, in_=kb[:]), "so", 16)

        signprep(w1Ts, o_sgn1, o_w1s, 16, 1024, "w1")
        signprep(w2Ts, o_sgn2, o_w2s, 8, 2048, "w2", prev="w1")

        @ssec
        def _(s):
            for nm in ["so", "sst0", "sst1", "skv", "siw", "ssg"]:
                s.wait_ge(p.sem(nm), p.cnt[nm])

    return p.nc


def prep_D(ins, cRes):
    import ml_dtypes
    identb = np.eye(128, dtype=np.float32).astype(ml_dtypes.bfloat16)
    identf = np.eye(128, dtype=np.float32)
    hh = np.array([[1.0]], np.float32)
    while hh.shape[0] < 128:
        hh = np.block([[hh, hh], [hh, -hh]])
    hadb = hh.astype(ml_dtypes.bfloat16)
    w1T = _cm(np.asarray(ins["L1_ffn_w1"], np.float32).T)
    w2T = _cm(np.asarray(ins["L1_ffn_w2"], np.float32).T)
    shared = {
        "wqi": _cm(np.asarray(ins["L1_attn_wq_idx"], np.float32)),
        "wq": _cm(np.asarray(ins["L1_attn_wq"], np.float32)),
        "wki": _cm(np.asarray(ins["L1_attn_wk_idx"], np.float32)),
        "wk": _cm(np.asarray(ins["L1_attn_wk"], np.float32)),
        "wv": _cm(np.asarray(ins["L1_attn_wv"], np.float32)),
        "wiw": _cm(np.asarray(ins["L1_attn_w_iw"], np.float32)),
        "hadb": hadb, "identb": identb, "identf": identf,
    }
    maps = []
    for c in range(NCORES):
        m = dict(shared)
        m["xn1b"] = _cm(cRes[c]["xn1b"])
        m["w1Ts"] = _cm(w1T[:, c * 1024:(c + 1) * 1024])
        m["w2Ts"] = _cm(w2T[c * 1024:(c + 1) * 1024, :])
        maps.append(m)
    return maps


def build_launch_E():
    p = Prog("E")
    nc = p.nc
    qih = p.din("qih", [D, R], BF16)
    qb = p.din("qb", [D, R], BF16)
    kih = p.din("kih", [128, T], BF16)
    kf = p.din("kf", [128, T], BF16)
    kvt = p.din("kvt", [T, 256], BF16)
    iwt = p.din("iwt", [R, 16])
    wo = p.din("wo", [D, D])
    identb = p.din("identb", [128, 128], BF16)
    negb = p.din("negb", [128, NT])

    o_s1f = p.dout("s1f", [D, R])

    SCALE = float(np.float32(HD ** -0.5))

    with p.ctx:
        QIH = p.sb("QIH", [128, 16 * 1024], BF16)
        QB = p.sb("QB", [128, 16 * 1024], BF16)
        KIH = p.sb("KIH", [128, T], BF16)
        KF = p.sb("KF", [128, T], BF16)
        KVT = p.sb("KVT", [128, 16 * 256], BF16)
        IWT = p.sb("IWT", [128, NT * 16])
        It = p.sb("It", [128, T])
        iotaL = p.sb("iotaL", [128, T])
        ck = p.sb("ck", [128, T])
        pen2 = p.sb("pen2", [128, T])
        scf = p.sb("scf", [128, T])
        Pb = p.sb("Pb", [128, T], BF16)
        PT = p.sb("PT", [128, T], BF16)
        OFA = p.sb("OFA", [128, 16 * 1024], BF16)
        m8 = p.sb("m8", [128, 8])
        rmx = p.sb("rmx", [128, 4])
        nbs = p.sb("nbs", [128, NT])
        cc = p.sb("cc", [128, 4])
        cid = p.sb("cid", [128, 128], BF16)
        wst = [p.sb(f"wst{i}", [128, D], BF16) for i in range(2)]
        ev = [p.sb(f"ev{i}", [128, 512]) for i in range(2)]

        psS = [p.ps(f"psS{i}", [128, 512]) for i in range(4)]
        psO = p.ps("psO", [128, 128])
        psT = p.ps("psT", [128, 128], BF16)
        psP = [p.ps(f"psP{i}", [128, 512]) for i in range(2)]

        blk = p.ctx.enter_context(nc.Block())
        for s in ["sx", "sv", "sa", "sg", "st", "smm", "sw0", "sw1",
                  "sev0", "sev1", "so"]:
            p.sem(s)

        def vsec(f):
            @blk.vector
            def _(v):
                f(p.chain(v, "sv"))

        def asec(f):
            @blk.scalar
            def _(a):
                f(p.chain(a, "sa"))

        tsec, ssec, gsec = blk.tensor, blk.sync, blk.gpsimd
        mk = {}

        @ssec
        def _(s):
            for dst, srcap in [
                (bass.AP(QIH, 0, [[16 * 1024, 128], [1024, 16], [1, 1024]]),
                 bass.AP(qih.tensor, 0, [[1024, 128], [128 * 1024, 16], [1, 1024]])),
                (bass.AP(QB, 0, [[16 * 1024, 128], [1024, 16], [1, 1024]]),
                 bass.AP(qb.tensor, 0, [[1024, 128], [128 * 1024, 16], [1, 1024]])),
                (KIH[:], kih), (KF[:], kf),
                (bass.AP(KVT, 0, [[16 * 256, 128], [256, 16], [1, 256]]),
                 bass.AP(kvt.tensor, 0, [[256, 128], [128 * 256, 16], [1, 256]])),
                (bass.AP(IWT, 0, [[NT * 16, 128], [16, NT], [1, 16]]),
                 bass.AP(iwt.tensor, 0, [[16, 128], [128 * 16, NT], [1, 16]])),
                (cid[:], identb), (nbs[:], negb),
            ]:
                p.inc(s.dma_start(out=dst, in_=srcap), "sx", 16)
            mk["consts"] = p.cnt["sx"]

        @gsec
        def _(g):
            g.iota(iotaL[:], [[-1, T]], base=0, channel_multiplier=1,
                   allow_small_or_imprecise_dtypes=True)
            inst = g.memset(cc[:, 0:1], -1.5e30)
            p.inc(inst, "sg")

        def f0(v):
            v.wait_ge(p.sem("sg"), p.cnt["sg"])
            v.memset(cc[:, 1:2], 1e30)
            v.memset(cc[:, 2:3], 1.0)
            v.memset(cc[:, 3:4], SCALE)
            mk["cc"] = p.cnt["sv"]
        vsec(f0)

        mm_after = {}
        for tt in range(NT):
            for h in range(H):
                for sg in range(4):
                    pp = psS[sg]

                    @tsec
                    def _(t, tt=tt, h=h, sg=sg, pp=pp):
                        if tt == 0 and h == 0 and sg == 0:
                            t.wait_ge(p.sem("sx"), mk["consts"])
                        t.wait_ge(p.sem("sv"), mk.get(("psSfree", sg), 0))
                        p.inc(t.matmul(pp[:],
                                       QIH[:, h * 1024 + tt * 128:
                                           h * 1024 + tt * 128 + 128],
                                       KIH[:, sg * 512:(sg + 1) * 512],
                                       start=True, stop=True), "smm")
                        mk[("zp", sg)] = p.cnt["smm"]

                    def facc(v, tt=tt, h=h, sg=sg, pp=pp):
                        v.wait_ge(p.sem("smm"), mk[("zp", sg)])
                        iwcol = IWT[:, tt * 16 + h: tt * 16 + h + 1]
                        sl = slice(sg * 512, (sg + 1) * 512)
                        if h == 0:
                            v.tensor_scalar(It[:, sl], pp[:], 0.0, iwcol,
                                            OP.max, OP.mult)
                        else:
                            v.tensor_scalar(scf[:, 0:512], pp[:], 0.0, iwcol,
                                            OP.max, OP.mult)
                            v.tensor_tensor(It[:, sl], It[:, sl],
                                            scf[:, 0:512], OP.add)
                        mk[("psSfree", sg)] = p.cnt["sv"]
                    vsec(facc)

            def fmask(v, tt=tt):
                v.wait_ge(p.sem("sv"), mk["cc"])
                v.tensor_scalar(ck[:], iotaL[:], nbs[:, tt:tt + 1], None,
                                OP.is_ge)
                v.tensor_tensor(It[:], It[:], ck[:], OP.mult)
                v.tensor_scalar(scf[:], ck[:], cc[:, 2:3], cc[:, 1:2],
                                OP.subtract, OP.mult)
                v.tensor_tensor(It[:], It[:], scf[:], OP.add)
                mk["masked"] = p.cnt["sv"]
            vsec(fmask)

            def fzap(v, tt=tt):
                for rnd in range(TOPK // 8):
                    v.max(out=m8[:], in_=It[:])
                    v.match_replace(out=It[:], in_to_replace=m8[:],
                                    in_values=It[:], imm_value=-2e30)
                mk["zapped"] = p.cnt["sv"]
            vsec(fzap)

            def fsel(v, tt=tt):
                v.tensor_scalar(scf[:], It[:], cc[:, 0:1], None, OP.is_le)
                v.tensor_tensor(scf[:], scf[:], ck[:], OP.mult)
                v.tensor_scalar(pen2[:], scf[:], cc[:, 2:3], cc[:, 1:2],
                                OP.subtract, OP.mult)
                mk["pen2"] = p.cnt["sv"]
                for sg in range(4):
                    mk[("psSfree", sg)] = p.cnt["sv"]
            vsec(fsel)

            for h in range(H):
                for sg in range(4):
                    pp = psS[sg]

                    @tsec
                    def _(t, tt=tt, h=h, sg=sg, pp=pp):
                        t.wait_ge(p.sem("sv"), mk.get(("psSfree", sg), 0))
                        p.inc(t.matmul(pp[:],
                                       QB[:, h * 1024 + tt * 128:
                                          h * 1024 + tt * 128 + 128],
                                       KF[:, sg * 512:(sg + 1) * 512],
                                       start=True, stop=True), "smm")
                        mk[("sc", sg)] = p.cnt["smm"]

                def fsc(v, tt=tt, h=h):
                    v.wait_ge(p.sem("sa"), mk.get("exp", 0))
                    for sg in range(4):
                        v.wait_ge(p.sem("smm"), mk[("sc", sg)])
                        sl = slice(sg * 512, (sg + 1) * 512)
                        v.scalar_tensor_tensor(scf[:, sl], psS[sg][:],
                                               cc[:, 3:4], pen2[:, sl],
                                               OP.mult, OP.add)
                        mk[("psSfree", sg)] = p.cnt["sv"]
                    v.tensor_reduce(rmx[:, 0:1], scf[:], AX, OP.max)
                    v.tensor_scalar(rmx[:, 1:2], rmx[:, 0:1], -1.0, None,
                                    OP.mult)
                    mk["sc_rdy"] = p.cnt["sv"]
                vsec(fsc)

                def fexp(a, tt=tt, h=h):
                    a.wait_ge(p.sem("sv"), mk["sc_rdy"])
                    a.wait_ge(p.sem("smm"), mk.get("o_mm", 0))
                    a.activation(Pb[:], scf[:], AF.Exp, bias=rmx[:, 1:2],
                                 accum_out=rmx[:, 2:3])
                    mk["exp"] = p.cnt["sa"]
                asec(fexp)

                def fnorm(v, tt=tt, h=h):
                    v.wait_ge(p.sem("sa"), mk["exp"])
                    v.wait_ge(p.sem("st"), p.cnt["st"])
                    v.reciprocal(rmx[:, 3:4], rmx[:, 2:3])
                    v.tensor_scalar_mul(PT[:], Pb[:], rmx[:, 3:4])
                    mk["pn"] = p.cnt["sv"]
                vsec(fnorm)

                abase = p.cnt["sa"]

                @tsec
                def _(t, tt=tt, h=h, abase=abase):
                    t.wait_ge(p.sem("sv"), mk["pn"])
                    for j2 in range(16):
                        t.wait_ge(p.sem("sa"), abase + j2 if j2 >= 1 else abase)
                        p.inc(t.transpose(psT[:],
                                          PT[:, j2 * 128:(j2 + 1) * 128],
                                          cid[:]), "st")
                stb = p.cnt["st"] - 16
                for j2 in range(16):
                    def fpe(a, j2=j2, stb=stb):
                        a.wait_ge(p.sem("st"), stb + j2 + 1)
                        a.copy(Pb[:, j2 * 128:(j2 + 1) * 128],
                               psT[:])
                        mk["ptr_ev"] = p.cnt["sa"]
                    asec(fpe)

                @tsec
                def _(t, tt=tt, h=h):
                    t.wait_ge(p.sem("sa"), mk["ptr_ev"])
                    t.wait_ge(p.sem("sa"), mk.get("o_ev", 0))
                    for j2 in range(16):
                        inst = t.matmul(psO[:],
                                        KVT[:, j2 * 256 + 128: j2 * 256 + 256],
                                        Pb[:, j2 * 128:(j2 + 1) * 128],
                                        start=(j2 == 0), stop=(j2 == 15))
                    p.inc(inst, "smm")
                    mk["o_mm"] = p.cnt["smm"]

                def foev(a, tt=tt, h=h):
                    a.wait_ge(p.sem("smm"), mk["o_mm"])
                    a.copy(OFA[:, h * 1024 + tt * 128: h * 1024 + tt * 128 + 128],
                           psO[:])
                    mk["o_ev"] = p.cnt["sa"]
                asec(foev)

        mk["ofa_all"] = p.cnt["sa"]

        for o in range(16):
            buf = o % 2

            @gsec
            def _(g, o=o, buf=buf):
                if o >= 2:
                    g.wait_ge(p.sem("smm"), mm_after[o - 2])
                srcap = bass.AP(wo.tensor, o * 128,
                                [[D, 128], [128 * D, 16], [1, 128]])
                dst = bass.AP(wst[buf], 0, [[D, 128], [128, 16], [1, 128]])
                p.inc(g.dma_start(out=dst, in_=srcap), f"sw{buf}", 16)

            for g2 in range(2):
                pp = psP[g2]

                @tsec
                def _(t, o=o, buf=buf, g2=g2, pp=pp):
                    if g2 == 0:
                        t.wait_ge(p.sem(f"sw{buf}"), 16 * (o // 2 + 1))
                        if o == 0:
                            t.wait_ge(p.sem("sa"), mk["ofa_all"])
                    if o >= 1:
                        t.wait_ge(p.sem(f"sev{g2}"), 16 * o)
                    for hh in range(16):
                        inst = t.matmul(pp[:], wst[buf][:, hh * 128:(hh + 1) * 128],
                                        OFA[:, hh * 1024 + g2 * 512:
                                            hh * 1024 + g2 * 512 + 512],
                                        start=(hh == 0), stop=(hh == 15))
                    p.inc(inst, "smm")
                    if g2 == 1:
                        mm_after[o] = p.cnt["smm"]
                mmc = p.cnt["smm"]

                def fev(a, pp=pp, g2=g2, mmc=mmc):
                    a.wait_ge(p.sem("smm"), mmc)
                    a.copy(ev[g2][:], pp[:])
                    mk["wo_ev"] = p.cnt["sa"]
                asec(fev)

                @ssec
                def _(s, o=o, g2=g2):
                    s.wait_ge(p.sem("sa"), mk["wo_ev"])
                    dst = o_s1f[o * 128:(o + 1) * 128, g2 * 512:(g2 + 1) * 512]
                    p.inc(s.dma_start(out=dst, in_=ev[g2][:]), f"sev{g2}", 16)

        @ssec
        def _(s):
            s.wait_ge(p.sem("sev0"), p.cnt["sev0"])
            s.wait_ge(p.sem("sev1"), p.cnt["sev1"])

    return p.nc


def prep_E(ins, dRes):
    import ml_dtypes
    identb = np.eye(128, dtype=np.float32).astype(ml_dtypes.bfloat16)
    woD = _cm(np.asarray(ins["L1_attn_wo"], np.float32))
    maps = []
    for c in range(NCORES):
        b, half = c // 2, c % 2
        kih = np.concatenate([dRes[2 * b]["kih"], dRes[2 * b + 1]["kih"]], axis=1)
        kff = np.concatenate([dRes[2 * b]["kf"], dRes[2 * b + 1]["kf"]], axis=1)
        kvt = np.concatenate([dRes[2 * b]["kvt"], dRes[2 * b + 1]["kvt"]], axis=0)
        negb = np.empty((128, NT), np.float32)
        for j in range(NT):
            negb[:, j] = -(half * 1024 + j * 128)
        m = {
            "qih": _cm(dRes[c]["qih"]), "qb": _cm(dRes[c]["qb"]),
            "kih": _cm(kih), "kf": _cm(kff), "kvt": _cm(kvt),
            "iwt": _cm(dRes[c]["iwt"]), "wo": woD,
            "identb": identb, "negb": negb,
        }
        maps.append(m)
    return maps


def build_launch_F():
    p = Prog("F")
    nc = p.nc
    x1f = p.din("x1f", [D, R])
    xn1f = p.din("xn1f", [D, R])
    s1f = p.din("s1f", [D, R])
    sgn1 = p.din("sgn1", [D, INNER], BF16)
    sgn2 = p.din("sgn2", [INNER, D], BF16)
    w1s8 = p.din("w1s8", [8, 1])
    w2s8 = p.din("w2s8", [8, 1])
    b1f = p.din("b1f", [128, 64])
    b2f = p.din("b2f", [128, 16])
    identf = p.din("identf", [128, 128])
    ones1r = p.din("ones1r", [1, 128])

    o_out = p.dout("outr", [R, D])
    hscr = p.dint("hscr", [INNER, R])

    with p.ctx:
        U = p.sb("U", [128, 16 * 1024])
        UB = p.sb("UB", [128, 64 * 512], BF16)
        XIB = UB
        wst = [p.sb(f"wst{i}", [128, 16 * 128], BF16) for i in range(2)]
        w2st = [p.sb("w2st0", [128, 64 * 128], BF16)]
        tv = p.sb("tv", [128, 1024])
        tu = p.sb("tu", [128, 1024])
        tw = p.sb("tw", [128, 1024])
        b0t = p.sb("b0t", [128, 1024], BF16)
        gs = p.sb("gs", [128, 512])
        st1 = p.sb("st1", [128, 512])
        st2 = p.sb("st2", [128, 512])
        ccols = p.sb("ccols", [128, 12])
        cidf = p.sb("cidf", [128, 128])
        c1r = p.sb("c1r", [1, 128])
        b1s = p.sb("b1s", [128, 64])
        b2s = p.sb("b2s", [128, 16])
        w8 = p.sb("w8", [8, 2])
        r1 = p.sb("r1", [1, 128])
        m1 = p.sb("m1", [1, 2])
        macc = p.sb("macc", [128, 1024])
        ms = p.sb("ms", [1, 1024])
        mrow = p.sb("mrow", [1, 1024])
        rr = p.sb("rr", [1, 1024])
        csr = p.sb("csr", [1, 1024])
        vsr = p.sb("vsr", [1, 1024])
        ms2, mrow2, rr2, csr2, vsr2 = ms, mrow, rr, csr, vsr
        tcol = p.sb("tcol", [128, 2])

        psQ = [p.ps(f"psQ{i}", [1, 512]) for i in range(2)]
        psH = [p.ps(f"psH{i}", [1, 512]) for i in range(2)]
        psB = p.ps("psB", [128, 512])
        psP = [p.ps(f"psP{i}", [128, 512]) for i in range(2)]
        psT = p.ps("psT", [128, 128])

        blk = p.ctx.enter_context(nc.Block())
        for s in ["sx", "sv", "sa", "sg", "st", "smm", "sw0", "sw1",
                  "sh0", "sh1", "shl0", "shl1", "sxs", "so", "sres"]:
            p.sem(s)

        def vsec(f):
            @blk.vector
            def _(v):
                f(p.chain(v, "sv"))

        def asec(f):
            @blk.scalar
            def _(a):
                f(p.chain(a, "sa"))

        tsec, ssec, gsec = blk.tensor, blk.sync, blk.gpsimd
        mk = {}

        @ssec
        def _(s):
            for dst, srcap in [(cidf[:], identf), (c1r[:], ones1r),
                               (b1s[:], b1f), (b2s[:], b2f),
                               (w8[0:8, 0:1], w1s8), (w8[0:8, 1:2], w2s8)]:
                p.inc(s.dma_start(out=dst, in_=srcap), "sx", 16)
            p.inc(s.dma_start(out=U[:],
                              in_=bass.AP(xn1f.tensor, 0,
                                          [[1024, 128], [128 * 1024, 16],
                                           [1, 1024]])), "sxs", 16)
            mk["consts"] = p.cnt["sx"]

        def f0(v):
            v.wait_ge(p.sem("sx"), mk["consts"])
            for i, val in enumerate([128.0, 64.0, 32.0, 16.0, 8.0, 4.0, 2.0,
                                     1.0, 0.0, -128.0, 127.0]):
                v.memset(ccols[:, i:i + 1], val)
            v.memset(tcol[:, 0:1], 1.0)
            mk["cc"] = p.cnt["sv"]
        vsec(f0)

        @tsec
        def _(t):
            t.wait_ge(p.sem("sv"), mk["cc"])
            p.inc(t.matmul(psT[0:1, 0:8], w8[0:8, 0:1], cidf[0:8, 0:8],
                           start=True, stop=True), "smm")
        mk["w8mm"] = p.cnt["smm"]

        def f2a(v):
            v.wait_ge(p.sem("smm"), mk["w8mm"])
            v.tensor_copy(r1[0:1, 0:8], psT[0:1, 0:8])
            v.tensor_reduce(m1[0:1, 0:1], r1[0:1, 0:8], AX, OP.add)
            v.tensor_scalar(m1[0:1, 0:1], m1[0:1, 0:1], 1.0 / (INNER * D),
                            None, OP.mult)
            mk["w1r"] = p.cnt["sv"]
        vsec(f2a)

        @tsec
        def _(t):
            t.wait_ge(p.sem("sv"), mk["w1r"])
            p.inc(t.matmul(psT[0:1, 0:8], w8[0:8, 1:2], cidf[0:8, 0:8],
                           start=True, stop=True), "smm")
        mk["w8mm2"] = p.cnt["smm"]

        def f2(v):
            v.wait_ge(p.sem("smm"), mk["w8mm2"])
            v.tensor_copy(r1[0:1, 0:8], psT[0:1, 0:8])
            v.tensor_reduce(m1[0:1, 1:2], r1[0:1, 0:8], AX, OP.add)
            v.tensor_scalar(m1[0:1, 1:2], m1[0:1, 1:2], 1.0 / (INNER * D),
                            None, OP.mult)
            mk["m1"] = p.cnt["sv"]
        vsec(f2)

        def quant_stats(XSRC, nchunk, msrow, mrowrow, rrow, csrow, vsrow,
                        eps, dimscale, mwcol, tag):
            for g2 in range(2):
                for cch in range(nchunk):
                    def fq(a, g2=g2, cch=cch):
                        if cch == 0 and g2 == 0:
                            a.wait_ge(p.sem("sxs"), mk.get((tag, "src"), 16))
                        a.wait_ge(p.sem("smm"), mk.get((tag, "qmmL"), 0))
                        a.wait_ge(p.sem("sv"), mk.get((tag, "mx"), 0))
                        a.activation(gs[:], XSRC[:, cch * 1024 + g2 * 512:
                                                 cch * 1024 + g2 * 512 + 512],
                                     AF.Square)
                        mk[(tag, "sq")] = p.cnt["sa"]
                    asec(fq)

                    @tsec
                    def _(t, g2=g2, cch=cch):
                        t.wait_ge(p.sem("sa"), mk[(tag, "sq")])
                        if cch == 0:
                            t.wait_ge(p.sem("sv"), mk["cc"])
                        p.inc(t.matmul(psQ[g2][:], tcol[:, 0:1], gs[:],
                                       start=(cch == 0), stop=(cch == nchunk - 1)),
                              "smm")
                        mk[(tag, "qmm", g2)] = p.cnt["smm"]
                        mk[(tag, "qmmL")] = p.cnt["smm"]

                    def fm(v, g2=g2, cch=cch):
                        v.wait_ge(p.sem("sa"), mk[(tag, "sq")])
                        dst = macc[:, g2 * 512:(g2 + 1) * 512]
                        if cch == 0:
                            v.tensor_copy(dst, gs[:])
                        else:
                            v.tensor_max(dst, dst, gs[:])
                        mk[(tag, "mx")] = p.cnt["sv"]
                    vsec(fm)

            for g2 in range(2):
                def fs(v, g2=g2):
                    v.wait_ge(p.sem("smm"), mk[(tag, "qmm", g2)])
                    sl = slice(g2 * 512, (g2 + 1) * 512)
                    v.tensor_scalar(msrow[0:1, sl], psQ[g2][0:1, :],
                                    1.0 / (nchunk * 128), None, OP.mult)
                    v.tensor_scalar(msrow[0:1, sl], msrow[0:1, sl], eps, None,
                                    OP.add)
                    mk[(tag, "ms", g2)] = p.cnt["sv"]
                vsec(fs)

                def fsa(a, g2=g2):
                    a.wait_ge(p.sem("sv"), mk[(tag, "ms", g2)])
                    sl = slice(g2 * 512, (g2 + 1) * 512)
                    a.activation(msrow[0:1, sl], msrow[0:1, sl], AF.Sqrt)
                    mk[(tag, "sd", g2)] = p.cnt["sa"]
                asec(fsa)

                def fsr(v, g2=g2):
                    v.wait_ge(p.sem("sa"), mk[(tag, "sd", g2)])
                    sl = slice(g2 * 512, (g2 + 1) * 512)
                    v.reciprocal(rrow[0:1, sl], msrow[0:1, sl])
                    v.tensor_scalar(rrow[0:1, sl], rrow[0:1, sl], dimscale,
                                    None, OP.mult)
                    mk[(tag, "rr", g2)] = p.cnt["sv"]
                vsec(fsr)

            for g2 in range(2):
                for tb in range(4):
                    @tsec
                    def _(t, g2=g2, tb=tb):
                        t.wait_ge(p.sem("sv"), mk[(tag, "mx")])
                        t.wait_ge(p.sem("sv"), mk.get((tag, "mxr"), mk["m1"]))
                        p.inc(t.transpose(psT[:],
                                          macc[:, g2 * 512 + tb * 128:
                                               g2 * 512 + tb * 128 + 128],
                                          cidf[:]), "st")
                        mk[(tag, "mxt")] = p.cnt["st"]

                    def fr(v, g2=g2, tb=tb):
                        v.wait_ge(p.sem("st"), mk[(tag, "mxt")])
                        v.tensor_reduce(tcol[:, 1:2], psT[:], AX, OP.max)
                        mk[(tag, "mxr")] = p.cnt["sv"]
                    vsec(fr)

                    @tsec
                    def _(t, g2=g2, tb=tb):
                        t.wait_ge(p.sem("sv"), mk[(tag, "mxr")])
                        p.inc(t.matmul(psT[0:1, 0:128], tcol[:, 1:2], cidf[:],
                                       start=True, stop=True), "smm")
                        mk[(tag, "mxc")] = p.cnt["smm"]

                    def fr2(v, g2=g2, tb=tb):
                        v.wait_ge(p.sem("smm"), mk[(tag, "mxc")])
                        sl = slice(g2 * 512 + tb * 128, g2 * 512 + tb * 128 + 128)
                        v.tensor_copy(mrowrow[0:1, sl], psT[0:1, 0:128])
                        mk[(tag, "mxrow")] = p.cnt["sv"]
                        mk[(tag, "mxr")] = p.cnt["sv"]
                    vsec(fr2)

            def fcs(a):
                a.wait_ge(p.sem("sv"), mk[(tag, "mxrow")])
                a.activation(mrowrow[:], mrowrow[:], AF.Sqrt)
                mk[(tag, "mxs")] = p.cnt["sa"]
            asec(fcs)

            def fc(v):
                v.wait_ge(p.sem("sa"), mk[(tag, "mxs")])
                v.wait_ge(p.sem("sv"), mk[(tag, "rr", 1)])
                v.wait_ge(p.sem("sv"), mk["m1"])
                v.tensor_tensor(mrowrow[:], mrowrow[:], rrow[:], OP.mult)
                v.tensor_scalar(mrowrow[:], mrowrow[:], EPS_Q, None, OP.max)
                v.reciprocal(vsrow[:], mrowrow[:])
                v.tensor_scalar(vsrow[:], vsrow[:], 127.0, None, OP.mult)
                v.tensor_tensor(vsrow[:], vsrow[:], rrow[:], OP.mult)
                v.tensor_scalar(csrow[:], mrowrow[:], 1.0 / 127.0, None, OP.mult)
                v.tensor_scalar_mul(csrow[:], csrow[:],
                                    m1[0:1, 0:1] if tag == "x" else m1[0:1, 1:2])
                mk[(tag, "scales")] = p.cnt["sv"]
            vsec(fc)

        quant_stats(U, 16, ms, mrow, rr, csr, vsr, EPS_RMS, float(D) ** -0.5,
                    None, "x")

        def quant_chunks(XSRC, DST, nchunk, vsrow, tag):
            for g2 in range(2):
                @tsec
                def _(t, g2=g2, _sv=None):
                    t.wait_ge(p.sem("sv"), mk[(tag, "scales")])
                    t.wait_ge(p.sem("sv"), p.cnt["sv"])
                    p.inc(t.matmul(psB[:], c1r[:],
                                   vsrow[0:1, g2 * 512:(g2 + 1) * 512],
                                   start=True, stop=True), "smm")
                    mk[(tag, "vsb", g2)] = p.cnt["smm"]

                for cch in range(nchunk):
                    def fq(v, g2=g2, cch=cch):
                        v.wait_ge(p.sem("smm"), mk[(tag, "vsb", g2)])
                        src = XSRC[:, cch * 1024 + g2 * 512:
                                   cch * 1024 + g2 * 512 + 512]
                        v.tensor_tensor(tv[:, 0:512], src, psB[:], OP.mult)
                        if cch == nchunk - 1:
                            mk[(tag, "qb_used")] = p.cnt["sv"] + 1
                        v.tensor_scalar(tu[:, 0:512], tv[:, 0:512], 128.5,
                                        None, OP.add)
                        for i in range(8):
                            v.tensor_scalar(tw[:, 0:512], tu[:, 0:512],
                                            ccols[:, i:i + 1], ccols[:, i:i + 1],
                                            OP.is_ge, OP.mult)
                            if i == 7:
                                v.tensor_copy(b0t[:, 0:512], tw[:, 0:512])
                            v.tensor_sub(tu[:, 0:512], tu[:, 0:512], tw[:, 0:512])
                        v.tensor_scalar(tv[:, 0:512], tv[:, 0:512], 128.5,
                                        None, OP.add)
                        v.tensor_sub(tv[:, 0:512], tv[:, 0:512], tu[:, 0:512])
                        v.tensor_scalar(tv[:, 0:512], tv[:, 0:512], -128.0,
                                        None, OP.add)
                        v.tensor_scalar(tu[:, 0:512], tu[:, 0:512],
                                        ccols[:, 8:9], None, OP.is_equal)
                        v.tensor_tensor(tu[:, 0:512], tu[:, 0:512],
                                        b0t[:, 0:512], OP.mult)
                        v.tensor_sub(tv[:, 0:512], tv[:, 0:512], tu[:, 0:512])
                        v.tensor_scalar(tv[:, 0:512], tv[:, 0:512],
                                        ccols[:, 9:10], ccols[:, 10:11],
                                        OP.max, OP.min)
                        v.tensor_copy(DST[:, cch * 1024 + g2 * 512:
                                          cch * 1024 + g2 * 512 + 512],
                                      tv[:, 0:512])
                        mk[(tag, "xi", g2, cch)] = p.cnt["sv"]
                    vsec(fq)
            mk[(tag, "xi_all")] = p.cnt["sv"]

        quant_chunks(U, XIB, 16, vsr, "x")

        mm_after = {}
        for o in range(64):
            buf = o % 2

            @gsec
            def _(g, o=o, buf=buf):
                if o >= 2:
                    g.wait_ge(p.sem("smm"), mm_after[o - 2])
                srcap = bass.AP(sgn1.tensor, o * 128,
                                [[INNER, 128], [128 * INNER, 16], [1, 128]])
                dst = bass.AP(wst[buf], 0, [[16 * 128, 128], [128, 16], [1, 128]])
                p.inc(g.dma_start(out=dst, in_=srcap), f"sw{buf}", 16)

            for g2 in range(2):
                pp = psP[g2]

                @tsec
                def _(t, o=o, buf=buf, g2=g2, pp=pp):
                    if g2 == 0:
                        t.wait_ge(p.sem(f"sw{buf}"), 16 * (o // 2 + 1))
                        if o == 0:
                            t.wait_ge(p.sem("sv"), mk[("x", "xi_all")])
                    t.wait_ge(p.sem(f"sh{g2}"), 16 * o)
                    for dd in range(16):
                        inst = t.matmul(pp[:], wst[buf][:, dd * 128:(dd + 1) * 128],
                                        XIB[:, dd * 1024 + g2 * 512:
                                            dd * 1024 + g2 * 512 + 512],
                                        start=(dd == 0), stop=(dd == 15))
                    p.inc(inst, "smm")
                    if g2 == 1:
                        mm_after[o] = p.cnt["smm"]
                mmc = p.cnt["smm"]

                if o == 0:
                    @tsec
                    def _(t, g2=g2):
                        t.wait_ge(p.sem("sv"), mk[("x", "scales")])
                        t.wait_ge(p.sem("sv"), p.cnt["sv"])
                        p.inc(t.matmul(psB[:], c1r[:],
                                       csr[0:1, g2 * 512:(g2 + 1) * 512],
                                       start=True, stop=True), "smm")
                        mk[("cs1b", g2)] = p.cnt["smm"]

                    def fsv(v, g2=g2):
                        v.wait_ge(p.sem("smm"), mk[("cs1b", g2)])
                        v.tensor_copy(st1[:] if g2 == 0 else st2[:], psB[:])
                        mk[("cs1s", g2)] = p.cnt["sv"]
                    vsec(fsv)

                def fep(v, o=o, g2=g2, pp=pp, mmc=mmc):
                    v.wait_ge(p.sem("smm"), mmc)
                    v.wait_ge(p.sem("sv"), mk[("cs1s", g2)])
                    v.wait_ge(p.sem("sa"), mk.get(("hgel", g2), 0))
                    v.wait_ge(p.sem("sh0"), mk.get(("hst", 0), 0))
                    v.wait_ge(p.sem("sh1"), mk.get(("hst", 1), 0))
                    v.tensor_tensor(gs[:], pp[:], st1[:] if g2 == 0 else st2[:],
                                    OP.mult)
                    mk[("hsc", g2)] = p.cnt["sv"]
                vsec(fep)

                def fge(a, o=o, g2=g2):
                    a.wait_ge(p.sem("sv"), mk[("hsc", g2)])
                    a.activation(gs[:], gs[:], AF.Gelu,
                                 bias=b1s[:, o:o + 1])
                    mk[("hgel", g2)] = p.cnt["sa"]
                asec(fge)

                @ssec
                def _(s, o=o, g2=g2):
                    s.wait_ge(p.sem("sa"), mk[("hgel", g2)])
                    p.inc(s.dma_start(out=hscr[o * 128:(o + 1) * 128,
                                             g2 * 512:(g2 + 1) * 512],
                                      in_=gs[:]), f"sh{g2}", 16)
                    mk[("hst", g2)] = p.cnt[f"sh{g2}"]

        mk[("h", "store_all")] = (p.cnt["sh0"], p.cnt["sh1"])

        for cch in range(64):
            dstbuf = tv if cch % 2 == 0 else tw

            @ssec
            def _(s, cch=cch, dstbuf=dstbuf):
                s.wait_ge(p.sem("sh0"), mk[("h", "store_all")][0])
                s.wait_ge(p.sem("sh1"), mk[("h", "store_all")][1])
                s.wait_ge(p.sem("sv"), mk.get(("hld_free", cch % 2), 0))
                p.inc(s.dma_start(out=dstbuf[:],
                                  in_=hscr[cch * 128:(cch + 1) * 128, :]),
                      f"shl{cch % 2}", 16)

            for g2 in range(2):
                def fq(a, g2=g2, cch=cch, dstbuf=dstbuf):
                    a.wait_ge(p.sem(f"shl{cch % 2}"), 16 * (cch // 2 + 1))
                    a.wait_ge(p.sem("smm"), mk.get(("h", "qmmL"), 0))
                    a.wait_ge(p.sem("sv"), mk.get(("h", "mx"), 0))
                    a.activation(gs[:], dstbuf[:, g2 * 512:(g2 + 1) * 512],
                                 AF.Square)
                    mk[("h", "sq")] = p.cnt["sa"]
                asec(fq)

                @tsec
                def _(t, g2=g2, cch=cch):
                    t.wait_ge(p.sem("sa"), mk[("h", "sq")])
                    if cch == 0:
                        t.wait_ge(p.sem("sv"), mk["cc"])
                    p.inc(t.matmul(psH[g2][:], tcol[:, 0:1], gs[:],
                                   start=(cch == 0), stop=(cch == 63)),
                          "smm")
                    mk[("h", "qmm", g2)] = p.cnt["smm"]
                    mk[("h", "qmmL")] = p.cnt["smm"]

                def fm(v, g2=g2, cch=cch, dstbuf=dstbuf):
                    v.wait_ge(p.sem("sa"), mk[("h", "sq")])
                    dst = macc[:, g2 * 512:(g2 + 1) * 512]
                    if cch == 0:
                        v.tensor_copy(dst, gs[:])
                    else:
                        v.tensor_max(dst, dst, gs[:])
                    if g2 == 1:
                        mk[("hld_free", cch % 2)] = p.cnt["sv"]
                    mk[("h", "mx")] = p.cnt["sv"]
                vsec(fm)

        for g2 in range(2):
            def fs(v, g2=g2):
                v.wait_ge(p.sem("smm"), mk[("h", "qmm", g2)])
                sl = slice(g2 * 512, (g2 + 1) * 512)
                v.tensor_scalar(ms2[0:1, sl], psH[g2][0:1, :],
                                1.0 / INNER, None, OP.mult)
                v.tensor_scalar(ms2[0:1, sl], ms2[0:1, sl], EPS_RMS, None,
                                OP.add)
                mk[("h", "ms", g2)] = p.cnt["sv"]
            vsec(fs)

            def fsa(a, g2=g2):
                a.wait_ge(p.sem("sv"), mk[("h", "ms", g2)])
                sl = slice(g2 * 512, (g2 + 1) * 512)
                a.activation(ms2[0:1, sl], ms2[0:1, sl], AF.Sqrt)
                mk[("h", "sd", g2)] = p.cnt["sa"]
            asec(fsa)

            def fsr(v, g2=g2):
                v.wait_ge(p.sem("sa"), mk[("h", "sd", g2)])
                sl = slice(g2 * 512, (g2 + 1) * 512)
                v.reciprocal(rr2[0:1, sl], ms2[0:1, sl])
                v.tensor_scalar(rr2[0:1, sl], rr2[0:1, sl],
                                float(INNER) ** -0.5, None, OP.mult)
                mk[("h", "rr", g2)] = p.cnt["sv"]
            vsec(fsr)

        for g2 in range(2):
            for tb in range(4):
                @tsec
                def _(t, g2=g2, tb=tb):
                    t.wait_ge(p.sem("sv"), mk[("h", "mx")])
                    t.wait_ge(p.sem("sv"), mk.get(("h", "mxr"), mk["m1"]))
                    p.inc(t.transpose(psT[:],
                                      macc[:, g2 * 512 + tb * 128:
                                           g2 * 512 + tb * 128 + 128],
                                      cidf[:]), "st")
                    mk[("h", "mxt")] = p.cnt["st"]

                def fr(v, g2=g2, tb=tb):
                    v.wait_ge(p.sem("st"), mk[("h", "mxt")])
                    v.tensor_reduce(tcol[:, 1:2], psT[:], AX, OP.max)
                    mk[("h", "mxr")] = p.cnt["sv"]
                vsec(fr)

                @tsec
                def _(t, g2=g2, tb=tb):
                    t.wait_ge(p.sem("sv"), mk[("h", "mxr")])
                    p.inc(t.matmul(psT[0:1, 0:128], tcol[:, 1:2], cidf[:],
                                   start=True, stop=True), "smm")
                    mk[("h", "mxc")] = p.cnt["smm"]

                def fr2(v, g2=g2, tb=tb):
                    v.wait_ge(p.sem("smm"), mk[("h", "mxc")])
                    sl = slice(g2 * 512 + tb * 128, g2 * 512 + tb * 128 + 128)
                    v.tensor_copy(mrow2[0:1, sl], psT[0:1, 0:128])
                    mk[("h", "mxrow")] = p.cnt["sv"]
                    mk[("h", "mxr")] = p.cnt["sv"]
                vsec(fr2)

        def fcs2(a):
            a.wait_ge(p.sem("sv"), mk[("h", "mxrow")])
            a.activation(mrow2[:], mrow2[:], AF.Sqrt)
            mk[("h", "mxs")] = p.cnt["sa"]
        asec(fcs2)

        def fc2(v):
            v.wait_ge(p.sem("sa"), mk[("h", "mxs")])
            v.wait_ge(p.sem("sv"), mk[("h", "rr", 1)])
            v.tensor_tensor(mrow2[:], mrow2[:], rr2[:], OP.mult)
            v.tensor_scalar(mrow2[:], mrow2[:], EPS_Q, None, OP.max)
            v.reciprocal(vsr2[:], mrow2[:])
            v.tensor_scalar(vsr2[:], vsr2[:], 127.0, None, OP.mult)
            v.tensor_tensor(vsr2[:], vsr2[:], rr2[:], OP.mult)
            v.tensor_scalar(csr2[:], mrow2[:], 1.0 / 127.0, None, OP.mult)
            v.tensor_scalar_mul(csr2[:], csr2[:], m1[0:1, 1:2])
            mk[("h", "scales")] = p.cnt["sv"]
        vsec(fc2)

        for g2 in range(2):
            @tsec
            def _(t, g2=g2):
                t.wait_ge(p.sem("sv"), mk[("h", "scales")])
                t.wait_ge(p.sem("sv"), p.cnt["sv"])
                p.inc(t.matmul(psB[:], c1r[:],
                               vsr2[0:1, g2 * 512:(g2 + 1) * 512],
                               start=True, stop=True), "smm")
                mk[("vs2b", g2)] = p.cnt["smm"]

            def fb2(v, g2=g2):
                v.wait_ge(p.sem("smm"), mk[("vs2b", g2)])
                v.tensor_copy(st1[:], psB[:])
                mk[("vs2s", g2)] = p.cnt["sv"]
            vsec(fb2)

            for cch in range(64):
                dstbuf = tv if cch % 2 == 0 else tw

                @ssec
                def _(s, cch=cch, g2=g2, dstbuf=dstbuf):
                    s.wait_ge(p.sem("sv"), mk.get(("hld2_free", cch % 2), 0))
                    if g2 == 0 and cch < 2:
                        s.wait_ge(p.sem("sv"), mk.get(("hld_free", cch % 2), 0))
                    if g2 == 1 and cch < 2:
                        s.wait_ge(p.sem("sv"), mk.get("res_done", 0))
                    p.inc(s.dma_start(out=dstbuf[:, 0:512],
                                      in_=hscr[cch * 128:(cch + 1) * 128,
                                               g2 * 512:(g2 + 1) * 512]),
                          f"shl{cch % 2}", 16)
                    mk[("hl2", cch % 2)] = p.cnt[f"shl{cch % 2}"]

                def fq(v, g2=g2, cch=cch, dstbuf=dstbuf):
                    v.wait_ge(p.sem(f"shl{cch % 2}"), mk[("hl2", cch % 2)])
                    v.wait_ge(p.sem("sv"), mk[("vs2s", g2)])
                    if g2 == 1 and cch == 0:
                        v.wait_ge(p.sem("smm"), mk[("mm2done", 0)])
                    v.wait_ge(p.sem("so"), mk.get("outst", 0))
                    src = dstbuf[:, 0:512]
                    v.tensor_tensor(tu[:, 0:512], src, st1[:], OP.mult)
                    v.tensor_scalar(tu[:, 512:1024], tu[:, 0:512], 128.5,
                                    None, OP.add)
                    u = tu[:, 512:1024]
                    for i in range(8):
                        v.tensor_scalar(gs[:], u, ccols[:, i:i + 1],
                                        ccols[:, i:i + 1], OP.is_ge, OP.mult)
                        if i == 7:
                            v.tensor_copy(b0t[:, 0:512], gs[:])
                        v.tensor_sub(u, u, gs[:])
                    v.tensor_scalar(tu[:, 0:512], tu[:, 0:512], 128.5, None,
                                    OP.add)
                    v.tensor_sub(tu[:, 0:512], tu[:, 0:512], u)
                    v.tensor_scalar(tu[:, 0:512], tu[:, 0:512], -128.0, None,
                                    OP.add)
                    v.tensor_scalar(u, u, ccols[:, 8:9], None, OP.is_equal)
                    v.tensor_tensor(u, u, b0t[:, 0:512], OP.mult)
                    v.tensor_sub(tu[:, 0:512], tu[:, 0:512], u)
                    v.tensor_scalar(tu[:, 0:512], tu[:, 0:512],
                                    ccols[:, 9:10], ccols[:, 10:11],
                                    OP.max, OP.min)
                    v.tensor_copy(UB[:, cch * 512:(cch + 1) * 512], tu[:, 0:512])
                    mk[("hld2_free", cch % 2)] = p.cnt["sv"]
                    mk[("ub", cch)] = p.cnt["sv"]
                vsec(fq)
            mk["ub_all"] = p.cnt["sv"]

            for o2 in range(16):
                buf = o2 % 2

                @gsec
                def _(g, o2=o2, g2=g2):
                    if (g2, o2 - 1) in mm_after:
                        g.wait_ge(p.sem("smm"), mm_after[(g2, o2 - 1)])
                    elif g2 == 1 and o2 == 0:
                        g.wait_ge(p.sem("smm"), mm_after[(0, 15)])
                    else:
                        g.wait_ge(p.sem("smm"), mm_after[63])
                    srcap = bass.AP(sgn2.tensor, o2 * 128,
                                    [[D, 128], [128 * D, 64], [1, 128]])
                    dst = bass.AP(w2st[0], 0,
                                  [[64 * 128, 128], [128, 64], [1, 128]])
                    p.inc(g.dma_start(out=dst, in_=srcap), "sw0", 16)
                    mk[("w2ld", g2, o2)] = p.cnt["sw0"]

                @tsec
                def _(t, o2=o2, g2=g2):
                    t.wait_ge(p.sem("sw0"), mk[("w2ld", g2, o2)])
                    t.wait_ge(p.sem("sv"), mk["ub_all"])
                    t.wait_ge(p.sem("sa"), mk.get(("g1", g2, o2 - 1), 0))
                    pp = psP[o2 % 2]
                    for dd in range(64):
                        inst = t.matmul(pp[:], w2st[0][:, dd * 128:(dd + 1) * 128],
                                        UB[:, dd * 512:(dd + 1) * 512],
                                        start=(dd == 0), stop=(dd == 63))
                    p.inc(inst, "smm")
                    mm_after[(g2, o2)] = p.cnt["smm"]
                mmc = p.cnt["smm"]

                if o2 == 0:
                    @tsec
                    def _(t, g2=g2):
                        t.wait_ge(p.sem("sv"), mk[("h", "scales")])
                        t.wait_ge(p.sem("sv"), p.cnt["sv"])
                        p.inc(t.matmul(psB[:], c1r[:],
                                       csr2[0:1, g2 * 512:(g2 + 1) * 512],
                                       start=True, stop=True), "smm")
                        mk[("cs2b", g2)] = p.cnt["smm"]

                    def fsv(v, g2=g2):
                        v.wait_ge(p.sem("smm"), mk[("cs2b", g2)])
                        v.tensor_copy(st2[:], psB[:])
                        mk[("cs2s", g2)] = p.cnt["sv"]
                    vsec(fsv)

                def fep(v, o2=o2, g2=g2, mmc=mmc):
                    v.wait_ge(p.sem("smm"), mmc)
                    v.wait_ge(p.sem("sv"), mk[("cs2s", g2)])
                    v.wait_ge(p.sem("sa"), mk.get(("g1a", g2, o2 - 1), 0))
                    pp = psP[o2 % 2]
                    v.tensor_tensor(gs[:], pp[:], st2[:], OP.mult)
                    mk[("g1v", g2, o2)] = p.cnt["sv"]
                vsec(fep)

                def fsi(a, o2=o2, g2=g2):
                    a.wait_ge(p.sem("sv"), mk[("g1v", g2, o2)])
                    a.activation(gs[:], gs[:], AF.Silu, bias=b2s[:, o2:o2 + 1])
                    mk[("g1", g2, o2)] = p.cnt["sa"]
                    mk[("g1a", g2, o2)] = p.cnt["sa"]
                asec(fsi)

                @ssec
                def _(s, o2=o2, g2=g2):
                    s.wait_ge(p.sem("sv"), mk.get(("resfree", o2 % 2), 0))
                    if o2 == 0:
                        s.wait_ge(p.sem("sv"), mk["ub_all"])
                    p.inc(s.dma_start(out=tv[:, 0:512],
                                      in_=x1f[o2 * 128:(o2 + 1) * 128,
                                              g2 * 512:(g2 + 1) * 512]),
                          "sres", 16)
                    p.inc(s.dma_start(out=tw[:, 0:512],
                                      in_=s1f[o2 * 128:(o2 + 1) * 128,
                                              g2 * 512:(g2 + 1) * 512]),
                          "sres", 16)
                    mk[("resld", o2, g2)] = p.cnt["sres"]

                def fre(v, o2=o2, g2=g2):
                    v.wait_ge(p.sem("sres"), mk[("resld", o2, g2)])
                    v.wait_ge(p.sem("sa"), mk[("g1", g2, o2)])
                    v.scalar_tensor_tensor(gs[:], gs[:], 1.0, tw[:, 0:512],
                                           OP.mult, OP.mult)
                    v.tensor_tensor(gs[:], gs[:], tv[:, 0:512], OP.add)
                    mk[("xo", g2, o2)] = p.cnt["sv"]
                    mk[("resfree", o2 % 2)] = p.cnt["sv"]
                    mk["res_done"] = p.cnt["sv"]
                vsec(fre)

                if o2 == 15:
                    mk[("mm2done", g2)] = p.cnt["smm"]
                for tb in range(4):
                    @tsec
                    def _(t, o2=o2, g2=g2, tb=tb):
                        t.wait_ge(p.sem("sv"), mk[("xo", g2, o2)])
                        t.wait_ge(p.sem("sa"), mk.get("outev", 0))
                        p.inc(t.transpose(psT[:],
                                          gs[:, tb * 128:(tb + 1) * 128],
                                          cidf[:]), "st")
                        mk["outtr"] = p.cnt["st"]

                    def fo(a, o2=o2, g2=g2, tb=tb):
                        a.wait_ge(p.sem("st"), mk["outtr"])
                        a.wait_ge(p.sem("so"), mk.get("outst", 0))
                        a.copy(tu[:, 0:128], psT[:])
                        mk["outev"] = p.cnt["sa"]
                    asec(fo)

                    @ssec
                    def _(s, o2=o2, g2=g2, tb=tb):
                        s.wait_ge(p.sem("sa"), mk["outev"])
                        r0 = g2 * 512 + tb * 128
                        p.inc(s.dma_start(out=o_out[r0:r0 + 128,
                                                    o2 * 128:(o2 + 1) * 128],
                                          in_=tu[:, 0:128]), "so", 16)
                        mk["outst"] = p.cnt["so"]

        @ssec
        def _(s):
            s.wait_ge(p.sem("so"), p.cnt["so"])

    return p.nc


def prep_F(ins, cRes, dRes, eRes):
    identf = np.eye(128, dtype=np.float32)
    ones1r = np.ones((1, 128), np.float32)
    sgn1 = np.concatenate([dRes[c]["sgn1"] for c in range(8)], axis=1)
    sgn2 = np.concatenate([dRes[c]["sgn2"] for c in range(8)], axis=0)
    w1s8 = np.stack([dRes[c]["w1s"].ravel() for c in range(8)]).reshape(8, 1)
    w2s8 = np.stack([dRes[c]["w2s"].ravel() for c in range(8)]).reshape(8, 1)
    b1f = _cm(np.asarray(ins["L1_ffn_b1"], np.float32)
              .reshape(64, 128).T)
    b2f = chunk_major_cols(np.asarray(ins["L1_ffn_b2"], np.float32))
    maps = []
    for c in range(NCORES):
        m = {
            "x1f": _cm(cRes[c]["x1f"]), "xn1f": _cm(cRes[c]["xn1f"]),
            "s1f": _cm(eRes[c]["s1f"]),
            "sgn1": sgn1, "sgn2": sgn2, "w1s8": w1s8, "w2s8": w2s8,
            "b1f": b1f, "b2f": b2f, "identf": identf, "ones1r": ones1r,
        }
        maps.append(m)
    return maps


_CACHE = {}


def _get(name, builder):
    if name not in _CACHE:
        _CACHE[name] = builder()
    return _CACHE[name]


def kernel(**inputs):
    core_ids = list(range(NCORES))
    aRes = run_bass_kernel_spmd(_get("A", build_launch_A), prep_A(inputs),
                                core_ids=core_ids).results
    bRes = run_bass_kernel_spmd(_get("B", build_launch_B), prep_B(inputs, aRes),
                                core_ids=core_ids).results
    cRes = run_bass_kernel_spmd(_get("C", build_launch_C),
                                prep_C(inputs, aRes, bRes),
                                core_ids=core_ids).results
    dRes = run_bass_kernel_spmd(_get("D", build_launch_D), prep_D(inputs, cRes),
                                core_ids=core_ids).results
    eRes = run_bass_kernel_spmd(_get("E", build_launch_E), prep_E(inputs, dRes),
                                core_ids=core_ids).results
    fRes = run_bass_kernel_spmd(_get("F", build_launch_F),
                                prep_F(inputs, cRes, dRes, eRes),
                                core_ids=core_ids).results
    out = np.concatenate([fRes[c]["outr"] for c in range(NCORES)], axis=0)
    return np.ascontiguousarray(out.reshape(B, T, D), dtype=np.float32)
